# version 17
# baseline (speedup 1.0000x reference)
import contextlib
import numpy as np
import ml_dtypes
import concourse.bass as bass
import concourse.mybir as mybir
from concourse.bass_utils import run_bass_kernel_spmd

F32 = mybir.dt.float32
BF16 = mybir.dt.bfloat16
I32 = mybir.dt.int32
U16 = mybir.dt.uint16
AF = mybir.ActivationFunctionType
ALU = mybir.AluOpType
AX = mybir.AxisListType

N_CORES = 8
T_CORE = 1024
T_PASS = 512
N_PASS = 2
N_TT = 4
D = 4096
KC = 32
I_DIM = 11008
NIT = 86
O_DIM = 4096
N_OS = 8
PLANE = 2752
GAMMA = 1.0 / np.sqrt(128.0)
EPS = 1e-5
DMAW = 16

ENGINES = ("sp", "pe", "dve", "act", "pool")
DEBUG = False


def _gu_blocks():
    blocks = []
    c = 0
    while c < I_DIM:
        bw = min(512, I_DIM - c)
        segs = []
        cc = c
        while cc < c + bw:
            j = cc // PLANE
            w0 = cc - j * PLANE
            nw = min(PLANE - w0, c + bw - cc)
            segs.append((j, w0, nw))
            cc += nw
        blocks.append((c, bw, segs))
        c += bw
    return blocks


GU_BLOCKS = _gu_blocks()
NB = len(GU_BLOCKS)


def _it_segs(it):
    segs = []
    c = 128 * it
    end = c + 128
    while c < end:
        j = c // PLANE
        w0 = c - j * PLANE
        nw = min(PLANE - w0, end - c)
        segs.append((j, w0, nw))
        c += nw
    return segs


IT_SEGS = [_it_segs(it) for it in range(NIT)]


def _cp_nat(cp):
    j = cp // PLANE
    return 4 * (cp - j * PLANE) + j


DMA_LANES = {"sp": 6, "ad": 6, "pl": 6, "sd": 3, "sd2": 3,
             "zg": 2, "zd": 2, "rp": 2, "su": 2}
DMA_SEM_NAMES = [b + str(i) for b, n in DMA_LANES.items() for i in range(n)]
COMPUTE_SEMS = ["pe", "dve", "act", "pool"]


class Sched:
    def __init__(self):
        self.ev = {e: [] for e in ENGINES}
        self.cnt = {n: 0 for n in DMA_SEM_NAMES + COMPUTE_SEMS}
        self.rr = {b: 0 for b in DMA_LANES}

    def emit(self, eng, fn, waits=(), inc=None, inc_n=1):
        if inc is not None:
            self.cnt[inc] += inc_n
        self.ev[eng].append((fn, tuple(waits), inc, inc_n))
        return self.cnt[inc] if inc is not None else None

    def val(self, sem):
        return self.cnt[sem]

    def snap(self):
        return dict(self.cnt)


def W(*pairs):
    out = []
    for p in pairs:
        if p is None:
            continue
        n, v = p
        if v:
            out.append((n, v))
    return out


def snap_waits(snap, bases):
    return [(b + str(i), snap[b + str(i)])
            for b in bases for i in range(DMA_LANES[b])
            if snap[b + str(i)]]


def build_nc():
    nc = bass.Bass("TRN2", target_bir_lowering=False, debug=False)

    def din(name, shape, dt):
        return nc.dram_tensor(name, shape, dt, kind="ExternalInput").ap()

    x_in = din("x", [T_CORE, D], F32)
    wq_in = din("wq", [D, D], F32)
    wk_in = din("wk", [D, 128], F32)
    wv_in = din("wv", [D, 128], F32)
    bq_in = din("bq", [128, 32], F32)
    bk_in = din("bk", [1, 128], F32)
    bv_in = din("bv", [1, 128], F32)
    gq_in = din("g_qw", [D, 1376], I32)
    gs_in = din("g_sc", [32, I_DIM], F32)
    gz_in = din("g_qz", [32, 1376], I32)
    uq_in = din("u_qw", [D, 1376], I32)
    us_in = din("u_sc", [32, I_DIM], F32)
    uz_in = din("u_qz", [32, 1376], I32)
    dq_in = din("d_qw", [I_DIM, 512], I32)
    ds_in = din("d_sc", [NIT, O_DIM], F32)
    dz_in = din("d_qz", [NIT, 512], I32)
    ident_in = din("ident", [128, 128], BF16)
    oneh_in = din("onehot", [128, 32 * 32], BF16)
    m_in = din("mmat", [I_DIM, NIT], BF16)

    out_t = nc.dram_tensor("out", [T_CORE, D], F32, kind="ExternalOutput").ap()
    if DEBUG:
        dbg = {
            "d_xn": nc.dram_tensor("d_xn", [128, KC * T_PASS], F32, kind="ExternalOutput").ap(),
            "d_wqkv": nc.dram_tensor("d_wqkv", [128, KC * 384], F32, kind="ExternalOutput").ap(),
            "d_qkvf": nc.dram_tensor("d_qkvf", [128, N_TT * 384], F32, kind="ExternalOutput").ap(),
            "d_oattn": nc.dram_tensor("d_oattn", [128, N_TT * 128], F32, kind="ExternalOutput").ap(),
            "d_stat": nc.dram_tensor("d_stat", [128, 64], F32, kind="ExternalOutput").ap(),
            "d_xs": nc.dram_tensor("d_xs", [32, T_PASS], F32, kind="ExternalOutput").ap(),
            "d_hs": nc.dram_tensor("d_hs", [NIT, T_PASS], F32, kind="ExternalOutput").ap(),
            "d_h": nc.dram_tensor("d_h", [128, NIT * T_PASS], F32, kind="ExternalOutput").ap(),
            "d_wdq": nc.dram_tensor("d_wdqg", [128, 2 * 512], BF16, kind="ExternalOutput").ap(),
            "d_h2": nc.dram_tensor("d_h2", [128, NIT * T_PASS], BF16, kind="ExternalOutput").ap(),
            "d_g0": nc.dram_tensor("d_g0", [128, 2 * 512], BF16, kind="ExternalOutput").ap(),
        }

    def dscr(name, shape, dt):
        return nc.dram_tensor(name, shape, dt, kind="Internal").ap()

    sprep_g_d = dscr("sprep_g", [32, I_DIM], BF16)
    sprep_u_d = dscr("sprep_u", [32, I_DIM], BF16)
    sprep_d_d = dscr("sprep_d", [NIT, O_DIM], BF16)
    zs_g_d = dscr("zsneg_g", [32, I_DIM], BF16)
    zs_u_d = dscr("zsneg_u", [32, I_DIM], BF16)
    zs_d_d = dscr("zsneg_d", [NIT, O_DIM], BF16)
    wqkv_d = dscr("wqkv_scr", [128, KC * 384], BF16)
    sd_exp = dscr("sd_exp", [NIT * 128, O_DIM], BF16)
    wd_perm = dscr("wd_perm", [I_DIM, 512], I32)
    GQT_OFF = {}
    off = 0
    for b, (c0b, bw, segs) in enumerate(GU_BLOCKS):
        ni = bw // 2
        for grp in range(8):
            GQT_OFF[(b, grp)] = off
            off += 128 * 4 * ni
    gqt = dscr("gq_tiled", [off], I32)
    uqt = dscr("uq_tiled", [off], I32)
    NICB = (NIT + 3) // 4
    WDT_OFF = {}
    SDT_OFF = {}
    o1, o2 = 0, 0
    for os_ in range(N_OS):
        for icb in range(NICB):
            nic = min(4, NIT - icb * 4)
            WDT_OFF[(os_, icb)] = o1
            SDT_OFF[(os_, icb)] = o2
            o1 += 128 * nic * 64
            o2 += 128 * nic * 512
    wdt = dscr("wd_tiled", [o1], I32)
    sdt = dscr("sd_tiled", [o2], BF16)
    bqs_d = dscr("bqs_scr", [128, 1], F32)

    ctx = contextlib.ExitStack()

    def sb(name, shape, dt):
        return ctx.enter_context(nc.sbuf_tensor(name, shape, dt))

    xn = sb("xn", [128, KC * T_PASS], BF16)
    h = sb("h", [128, NIT * T_PASS], BF16)
    ident_s = sb("ident_s", [128, 128], BF16)
    oneh_s = sb("oneh_s", [128, 32 * 32], BF16)
    bias3 = sb("bias3", [128, 384], F32)
    xs_bf = sb("xs_bf", [32, T_PASS], BF16)
    hs_bf = sb("hs_bf", [NIT, T_PASS], BF16)
    o_attn = sb("o_attn", [128, N_TT * 128], F32)
    qkvf = sb("qkvf", [128, N_TT * 384], F32)
    stat = sb("stat", [128, 64], F32)
    mring = sb("mring", [128, 4 * NIT], BF16)
    kab = sb("kab", [128, 4], F32)
    ones_c = sb("ones_c", [128, 1], F32)
    sexp = sb("sexp", [128, 2 * 128], F32)

    zring_gu = sb("zring_gu", [32, 4 * 512], BF16)
    zring_d = sb("zring_d", [NIT, 2 * 512], BF16)
    wdq_gu = sb("wdq_gu", [128, 4 * 512], BF16)
    wdq_dn = sb("wdq_dn", [128, 3 * 512], BF16)
    plane_gu = sb("plane_gu", [128, 2 * 512], U16)
    plane_dn = sb("plane_dn", [128, 512], U16)

    UA = sb("UA", [128, 8192], I32)
    UAf = UA[:].bitcast(F32)
    UAb = UA[:].bitcast(BF16)
    UAu = UA[:].bitcast(U16)
    xa = UAf[:, 0:2048]
    xb = UAf[:, 2048:4096]
    junk = UAf[:, 4096:6144]
    xn_tm = UAb[:, 12288:16384]
    raw_gu = UA[:][:, 0:4096]
    sbc_gu = UAb[:, 8192:12288]
    raw_d = UA[:][:, 0:512]
    sbc_d = UAb[:, 1024:5120]
    xsl = UAf[:, 2560:3584]
    osl = UAf[:, 3584:4608]
    sgb = UAf[:, 4608:5632]
    sil = UAf[:, 6144:7168]
    sil2 = UAf[:, 7168:8192]

    dbg_w = UAb[:, 14336:15360]
    dbg_g = UAb[:, 15360:16384]
    st_sc = UAf[:, 0:1376]
    st_sc2 = UAf[:, 1376:2752]
    st_qz = UA[:][:, 768:2144]
    st_zp = UAu[:, 4288:7040]
    st_zc = UAb[:, 7040:9792]

    UB = sb("UB", [128, 6144], I32)
    UBb = UB[:].bitcast(BF16)
    UBf = UB[:].bitcast(F32)
    sprep_sb = UBb[:, 0:I_DIM]
    wqkv = UBb[:, 0:KC * 384]
    sc_t = UBf[:, 0:1024]
    et_t = UBf[:, 1024:2048]
    zn_t = UBf[:, 2048:2816]
    sil3 = UBf[:, 2816:3840]
    psg = UBf[:, 3840:5888]

    pb = [ctx.enter_context(nc.psum_tensor(f"pb{i}", [128, 512], F32)) for i in range(8)]

    S = Sched()
    sems = {n: ctx.enter_context(nc.semaphore("sem_" + n))
            for n in DMA_SEM_NAMES + COMPUTE_SEMS}

    def dma(queue, dst, src, waits=(), sem=None):
        eng_name = {"sp": "sp", "ad": "act", "pl": "pool"}[queue]
        base = sem or queue
        lane = base + str(S.rr[base] % DMA_LANES[base])
        S.rr[base] += 1
        prev = S.cnt[lane]
        wl = list(waits)
        if prev:
            wl.append((lane, prev))

        def fn(eng, dst=dst, src=src):
            return eng.dma_start(out=dst, in_=src)
        return (lane, S.emit(eng_name, fn, waits=wl, inc=lane, inc_n=1))

    def pe(fn, waits=(), inc=False):
        return S.emit("pe", fn, waits=waits, inc="pe" if inc else None)

    def dve(fn, waits=(), inc=True):
        return S.emit("dve", fn, waits=waits, inc="dve" if inc else None)

    def act(fn, waits=(), inc=True):
        return S.emit("act", fn, waits=waits, inc="act" if inc else None)

    def pool(fn, waits=(), inc=True):
        return S.emit("pool", fn, waits=waits, inc="pool" if inc else None)

    def dfence(n=2):
        for _ in range(n - 1):
            S.emit("dve", lambda eng: eng.engine_nop(), inc="dve")
        return S.emit("dve", lambda eng: eng.engine_nop(), inc="dve")

    def afence(n=2):
        for _ in range(n - 1):
            S.emit("act", lambda eng: eng.mul(kab[0:1, 2:3], kab[0:1, 2:3], 1.0),
                   inc="act")
        return S.emit("act", lambda eng: eng.mul(kab[0:1, 2:3], kab[0:1, 2:3], 1.0),
                      inc="act")

    def plfence(n=2):
        for _ in range(n - 1):
            S.emit("pool", lambda eng: eng.engine_nop(), inc="pool")
        return S.emit("pool", lambda eng: eng.engine_nop(), inc="pool")

    def tiny(fn, waits=()):
        S.emit("dve", fn, waits=waits, inc="dve")
        S.emit("dve", lambda eng: eng.engine_nop(), inc="dve")
        return S.emit("dve", lambda eng: eng.engine_nop(), inc="dve")

    S.emit("dve", lambda eng: eng.memset(ones_c[:], 1.0), inc="dve")
    ones_ready = S.emit("dve", lambda eng: eng.engine_nop(), inc="dve")

    for gi_, (sc_in, qz_inp, sp_dram, zs_dram) in enumerate(
            [(gs_in, gz_in, sprep_g_d, zs_g_d), (us_in, uz_in, sprep_u_d, zs_u_d)]):
        v_sp = None
        ad_bar = S.snap()
        dve_bar = S.val("dve")
        for c4 in range(4):
            v_l = dma("sp", st_sc[0:32, :],
                      sc_in[:, c4 * 2752: c4 * 2752 + 1376],
                      waits=W(("dve", v_sp if v_sp else dve_bar)) +
                            (snap_waits(ad_bar, ["ad"]) if c4 == 0 else []), sem="su")
            v_l2 = dma("sp", st_sc2[0:32, :],
                       sc_in[:, c4 * 2752 + 1376:(c4 + 1) * 2752],
                       waits=W(("dve", v_sp if v_sp else dve_bar)), sem="su")
            for j in range(4):
                src1 = st_sc[0:32, :].rearrange("p (w f) -> p w f", f=4)[:, :, j]
                src2 = st_sc2[0:32, :].rearrange("p (w f) -> p w f", f=4)[:, :, j]
                d0 = j * PLANE + c4 * 688
                dve(lambda eng, s=src1, d0=d0: eng.tensor_copy(
                    sprep_sb[0:32, d0:d0 + 344], s),
                    waits=W(v_l))
                v_sp = dve(lambda eng, s=src2, d0=d0: eng.tensor_copy(
                    sprep_sb[0:32, d0 + 344:d0 + 688], s),
                    waits=W(v_l2))
        v_spw = dma("ad", sp_dram, sprep_sb[0:32, :], waits=W(("dve", v_sp)))
        v_qz = dma("sp", st_qz[0:32, :], qz_inp, waits=W(("dve", v_sp)), sem="su")
        qz16 = st_qz[0:32, :].bitcast(U16)
        v_zsw = None
        for j in range(4):
            dve(lambda eng, q=qz16, j=j: eng.tensor_scalar(
                st_zp[0:32, :], q, 4 * j, 15,
                ALU.logical_shift_right, ALU.bitwise_and),
                waits=W(v_qz, v_zsw))
            v_zs = dve(lambda eng, j=j: eng.scalar_tensor_tensor(
                st_zc[0:32, :], st_zp[0:32, :], -1.0,
                sprep_sb[0:32, j * PLANE:(j + 1) * PLANE], ALU.mult, ALU.mult))
            v_zsw = dma("ad", zs_dram[:, j * PLANE:(j + 1) * PLANE],
                        st_zc[0:32, :], waits=W(("dve", v_zs)))

    v_dsp = None
    chunk_v = [None, None, None]
    for ck in range(3):
        n0 = ck * 1376
        nn = min(1376, O_DIM - n0)
        v_l = dma("sp", st_sc[0:NIT, 0:nn], ds_in[:, n0:n0 + nn],
                  waits=W(("dve", v_dsp), v_zsw), sem="su")
        for s_ in range(8):
            for j in range(4):
                mlo = max(0, -(-(n0 - 512 * s_ - j) // 4))
                mhi = min(128, (n0 + nn - 512 * s_ - j + 3) // 4)
                if mlo >= mhi:
                    continue
                w0 = 512 * s_ + 4 * mlo + j - n0
                srcv = st_sc[0:NIT, w0:w0 + 4 * (mhi - mlo - 1) + 1:4]
                d0 = 512 * s_ + 128 * j + mlo
                v_dsp = dve(lambda eng, s=srcv, d0=d0, n=mhi - mlo:
                            eng.tensor_copy(sprep_sb[0:NIT, d0:d0 + n], s),
                            waits=W(v_l))
    v_dsp = dfence()
    v_spdw = dma("ad", sprep_d_d, sprep_sb[0:NIT, 0:O_DIM], waits=W(("dve", v_dsp)))
    v_dqz = dma("sp", st_qz[0:NIT, 0:512], dz_in, waits=W(("dve", v_dsp)), sem="su")
    qzd16 = st_qz[0:NIT, 0:512].bitcast(U16)
    v_zdw = None
    for s_ in range(8):
        for j in range(4):
            dve(lambda eng, s_=s_, j=j: eng.tensor_scalar(
                st_zp[0:NIT, 0:128], qzd16[:, 128 * s_:128 * s_ + 128],
                4 * j, 15, ALU.logical_shift_right, ALU.bitwise_and),
                waits=W(v_dqz, v_zdw))
            dfence(3)
            v_zd = dve(lambda eng, s_=s_, j=j: eng.scalar_tensor_tensor(
                st_zc[0:NIT, 0:128], st_zp[0:NIT, 0:128], -1.0,
                sprep_sb[0:NIT, s_ * 512 + j * 128: s_ * 512 + (j + 1) * 128],
                ALU.mult, ALU.mult))
            v_zd = dfence()
            v_zdw = dma("ad", zs_d_d[:, s_ * 512 + j * 128: s_ * 512 + (j + 1) * 128],
                        st_zc[0:NIT, 0:128], waits=W(("dve", v_zd)))

    rrq = ["sp", "ad", "pl"]
    rri = [0]

    def dma_rr(dst, srcap, waits=()):
        q = rrq[rri[0] % 3]
        rri[0] += 1
        return dma(q, dst, srcap, waits=waits)

    v_sdx_all = []
    for ic in range(NIT):
        poff = 0
        for (j, w0, nw) in IT_SEGS[ic]:
            pos = 0
            while pos < nw:
                gg = (4 * (w0 + pos) + j) // 128
                run = min(nw - pos, 32 - ((w0 + pos) % 32))
                srcap = sprep_d_d[gg:gg + 1, :].broadcast_to([run, O_DIM])
                dst = sd_exp[ic * 128 + poff + pos: ic * 128 + poff + pos + run, :]
                v_sdx_all.append(dma_rr(dst, srcap, waits=W(v_spdw)))
                pos += run
            poff += nw
    v_sdx_snap = S.snap()
    v_wdp_all = []
    for j in range(4):
        srcap = dq_in.rearrange("(w f) n -> w f n", f=4)[:, j, :]
        dst = wd_perm[j * PLANE:(j + 1) * PLANE, :]
        v_wdp_all.append(dma_rr(dst, srcap))
    v_wdp_snap = S.snap()

    v_gqt = {}
    for mat, (q_inp, t_out) in enumerate([(gq_in, gqt), (uq_in, uqt)]):
        for b, (c0b, bw, segs) in enumerate(GU_BLOCKS):
            ni_tot = bw // 2
            for grp in range(8):
                offp = GQT_OFF[(b, grp)]
                woff = 0
                v = None
                vv = []
                for (j, w0, nw) in segs:
                    i0, ni = w0 // 2, nw // 2
                    srcap = q_inp.rearrange("(c p) n -> c p n", p=128)[
                        grp * 4:(grp + 1) * 4, :, i0:i0 + ni].transpose([1, 0, 2])
                    dstap = t_out[offp:offp + 128 * 4 * ni_tot].rearrange(
                        "(p c n) -> p c n", p=128, c=4)[:, :, woff:woff + ni]
                    vv.append(dma_rr(dstap, srcap))
                    woff += ni
                v_gqt[(mat, b, grp)] = vv
    v_wdt = {}
    v_sdt = {}
    for os_ in range(N_OS):
        for icb in range(NICB):
            nic = min(4, NIT - icb * 4)
            srcap = wd_perm.rearrange("(a p) n -> p a n", p=128)[
                :, icb * 4:icb * 4 + nic, os_ * 64:(os_ + 1) * 64]
            dstap = wdt[WDT_OFF[(os_, icb)]:WDT_OFF[(os_, icb)] + 128 * nic * 64
                        ].rearrange("(p c n) -> p c n", p=128, c=nic)
            v_wdt[(os_, icb)] = dma_rr(dstap, srcap,
                                       waits=snap_waits(v_wdp_snap, rrq))
            srcap2 = sd_exp.rearrange("(a p) n -> p a n", p=128)[
                :, icb * 4:icb * 4 + nic, os_ * 512:(os_ + 1) * 512]
            dstap2 = sdt[SDT_OFF[(os_, icb)]:SDT_OFF[(os_, icb)] + 128 * nic * 512
                         ].rearrange("(p c n) -> p c n", p=128, c=nic)
            v_sdt[(os_, icb)] = dma_rr(dstap2, srcap2,
                                       waits=snap_waits(v_sdx_snap, rrq))

    v_id = dma("sp", ident_s[:], ident_in)
    v_oh = dma("sp", oneh_s[:], oneh_in)

    v_red = None
    for kc in range(KC):
        wl = W(v_zdw) if kc == 0 else W(("dve", v_red))
        v_l = dma("ad", xa, wq_in[kc * 128:(kc + 1) * 128, 0:2048], waits=wl)
        v_l2 = dma("ad", xb, wq_in[kc * 128:(kc + 1) * 128, 2048:4096])
        v_l3 = dma("ad", junk[:, 0:128], wk_in[kc * 128:(kc + 1) * 128, :])
        v_l4 = dma("ad", junk[:, 128:256], wv_in[kc * 128:(kc + 1) * 128, :])
        tiny(lambda eng: eng.reduce_sum(
            junk[:, 512:576].unsqueeze(2),
            xa.rearrange("p (d f) -> p d f", f=32), AX.X),
            waits=W(v_l))
        v_r2 = tiny(lambda eng: eng.reduce_sum(
            junk[:, 576:640].unsqueeze(2),
            xb.rearrange("p (d f) -> p d f", f=32), AX.X),
            waits=W(v_l2))
        dve(lambda eng, kc=kc: eng.tensor_scalar(
            wqkv[:, kc * 384: kc * 384 + 128], junk[:, 512:640],
            GAMMA, None, ALU.mult), waits=W(("dve", v_r2)))
        v_red = dve(lambda eng, kc=kc: eng.tensor_copy(
            wqkv[:, kc * 384 + 128: kc * 384 + 384], junk[:, 0:256]),
            waits=W(v_l4))
    v_red = dfence()
    v_wqkv_w = dma("ad", wqkv_d, wqkv[:], waits=W(("dve", v_red)))

    v_bq = dma("ad", xa[:, 0:32], bq_in, waits=W(("dve", v_red)))
    v_br = tiny(lambda eng: eng.reduce_sum(xb[:, 0:1].unsqueeze(2),
                                           xa[:, 0:32].unsqueeze(1), AX.X),
                waits=W(v_bq))
    v_bqs = tiny(lambda eng: eng.tensor_scalar(
        xb[:, 1:2], xb[:, 0:1], GAMMA, None, ALU.mult),
        waits=W(("dve", v_br)))
    v_bqsw = dma("ad", bqs_d, xb[:, 1:2], waits=W(("dve", v_bqs)))
    v_b1 = dma("sp", bias3[:, 0:128],
               bqs_d.rearrange("p one -> one p").partition_broadcast(128).squeeze(1),
               waits=W(v_bqsw))
    v_b2 = dma("sp", bias3[:, 128:256], bk_in.partition_broadcast(128).squeeze(1))
    v_b3 = dma("sp", bias3[:, 256:384], bv_in.partition_broadcast(128).squeeze(1))
    setup_end = S.snap()

    bank_guard = [[] for _ in range(8)]
    gu_raw_free = [None] * 4
    gu_sbc_free = [None] * 4
    gu_wdq_free = [None] * 4
    gu_z_free = [None] * 4
    dn_raw_free = [None] * 2
    dn_sbc_free = [None] * 2
    dn_wdq_free = [None] * 3
    dn_z_free = [None] * 2
    m_free = [None] * 4
    sil_free = [None] * 2
    psg_free = [None] * 2
    xsl_free = [None] * 2
    osl_free = [None] * 2
    sgb_free = [None] * 2
    sexp_free = [None] * 2
    prev_pass_end = None

    for p in range(N_PASS):
        t0 = p * T_PASS

        xn_ready_tt = [None] * N_TT
        half_free = [setup_end["act"], setup_end["act"]]
        tm_free = [None, None]
        for tt in range(N_TT):
            r0 = t0 + tt * 128
            half_loads = []
            v_sq = [None, None]
            for hf in range(2):
                buf = (xa, xb)[hf]
                wl = W(("act", half_free[hf]))
                if tt == 0:
                    if p == 0:
                        wl = wl + snap_waits(setup_end, ["sp", "su"]) + W(("dve", setup_end["dve"]))
                    else:
                        wl = wl + W(("dve", prev_pass_end["dve"]),
                                    ("pool", prev_pass_end["pool"])) + \
                            snap_waits(prev_pass_end, ["pl", "sd", "sd2"])
                v_x = dma("ad", buf,
                          x_in[r0:r0 + 128, hf * 2048:(hf + 1) * 2048], waits=wl)
                half_loads.append(v_x)
                dve(lambda eng, b=buf, tt=tt, hf=hf: eng.reduce_sum(
                    stat[:, tt * 16 + hf: tt * 16 + hf + 1].unsqueeze(2),
                    b.unsqueeze(1), AX.X),
                    waits=W(v_x))
                jw = W(v_x)
                if p == 0 and tt == 0:
                    jw = jw + snap_waits(setup_end, ["sp", "su"])
                v_sq[hf] = act(lambda eng, b=buf, tt=tt, hf=hf: eng.activation(
                    junk, b, AF.Square,
                    accum_out=stat[:, tt * 16 + 2 + hf: tt * 16 + 3 + hf]),
                    waits=jw)
            c0 = tt * 16
            v_c = tiny(lambda eng, c0=c0: eng.tensor_tensor(
                stat[:, c0 + 4:c0 + 5], stat[:, c0:c0 + 1],
                stat[:, c0 + 1:c0 + 2], ALU.add),
                waits=W(("act", v_sq[1])))
            v_c = tiny(lambda eng, c0=c0: eng.tensor_tensor(
                stat[:, c0 + 5:c0 + 6], stat[:, c0 + 2:c0 + 3],
                stat[:, c0 + 3:c0 + 4], ALU.add), waits=W(("dve", v_c)))
            v_c = tiny(lambda eng, c0=c0: eng.tensor_scalar(
                stat[:, c0 + 6:c0 + 7], stat[:, c0 + 4:c0 + 5],
                1.0 / D, None, ALU.mult), waits=W(("dve", v_c)))
            v_c = tiny(lambda eng, c0=c0: eng.tensor_scalar(
                stat[:, c0 + 7:c0 + 8], stat[:, c0 + 5:c0 + 6],
                1.0 / D, None, ALU.mult), waits=W(("dve", v_c)))
            v_c = tiny(lambda eng, c0=c0: eng.tensor_tensor(
                stat[:, c0 + 8:c0 + 9], stat[:, c0 + 6:c0 + 7],
                stat[:, c0 + 6:c0 + 7], ALU.mult), waits=W(("dve", v_c)))
            v_c = tiny(lambda eng, c0=c0: eng.tensor_tensor(
                stat[:, c0 + 9:c0 + 10], stat[:, c0 + 7:c0 + 8],
                stat[:, c0 + 8:c0 + 9], ALU.subtract), waits=W(("dve", v_c)))
            v_var = tiny(lambda eng, c0=c0: eng.tensor_scalar(
                stat[:, c0 + 10:c0 + 11], stat[:, c0 + 9:c0 + 10],
                EPS, None, ALU.add), waits=W(("dve", v_c)))
            act(lambda eng, c0=c0: eng.activation(
                stat[:, c0 + 11:c0 + 12], stat[:, c0 + 10:c0 + 11], AF.Ln),
                waits=W(("dve", v_var)))
            v_rstd = act(lambda eng, c0=c0: eng.activation(
                stat[:, c0 + 12:c0 + 13], stat[:, c0 + 11:c0 + 12],
                AF.Exp, scale=-0.5))
            v_nb = tiny(lambda eng, c0=c0: eng.scalar_tensor_tensor(
                stat[:, c0 + 13:c0 + 14], stat[:, c0 + 6:c0 + 7], -1.0,
                stat[:, c0 + 12:c0 + 13], ALU.mult, ALU.mult),
                waits=W(("act", v_rstd)))
            last_ev = None
            for hf in range(2):
                sl = hf
                wl = W(("dve", v_nb), half_loads[hf],
                       ("pe", tm_free[sl]))
                v_tm = act(lambda eng, b=(xa, xb)[hf], c0=c0, sl=sl: eng.activation(
                    xn_tm[:, sl * 2048:(sl + 1) * 2048], b, AF.Identity,
                    bias=stat[:, c0 + 13:c0 + 14], scale=stat[:, c0 + 12:c0 + 13]),
                    waits=wl)
                v_tm = afence()
                half_free[hf] = v_tm
                for k2 in range(16):
                    kc = hf * 16 + k2
                    bank = kc % 4
                    pview = pb[bank][:].bitcast(BF16)
                    g = bank_guard[bank]
                    bank_guard[bank] = []
                    v_mm = pe(lambda eng, pv=pview, k2=k2, sl=sl:
                              eng.transpose(pv[:, 0:128],
                                            xn_tm[:, sl * 2048 + k2 * 128:
                                                  sl * 2048 + (k2 + 1) * 128],
                                            ident_s[:]),
                              waits=W(("act", v_tm)) + g, inc=True)
                    v_ev = dve(lambda eng, pv=pview, kc=kc, tt=tt:
                               eng.tensor_copy(
                                   xn[:, kc * T_PASS + tt * 128:
                                      kc * T_PASS + (tt + 1) * 128],
                                   pv[:, 0:128]),
                               waits=W(("pe", v_mm)))
                    bank_guard[bank].append(("dve", v_ev))
                    last_ev = v_ev
                tm_free[sl] = S.val("pe")
            xn_ready_tt[tt] = dfence()
        xn_done = S.val("dve")

        for kc in range(KC):
            g = bank_guard[2] if kc == 0 else []
            ow = snap_waits(setup_end, ["sp"]) if (p == 0 and kc == 0) else []
            v_mm = pe(lambda eng, kc=kc: eng.matmul(
                pb[2][0:32, :], oneh_s[:, kc * 32:(kc + 1) * 32],
                xn[:, kc * T_PASS:(kc + 1) * T_PASS],
                start=(kc == 0), stop=(kc == KC - 1)),
                waits=W(("dve", xn_done)) + g + ow, inc=(kc == KC - 1))
        v_xs = dve(lambda eng: eng.tensor_copy(xs_bf[:], pb[2][0:32, :]),
                   waits=W(("pe", v_mm)))
        v_xs = dfence()
        bank_guard[2] = [("dve", v_xs)]

        if p > 0:
            v_wl = dma("ad", wqkv[:], wqkv_d,
                       waits=W(("dve", prev_pass_end["dve"]),
                               ("pool", prev_pass_end["pool"]),
                               ("act", prev_pass_end["act"])))
            wq_wait = W(v_wl)
        else:
            wq_wait = W(("dve", v_red))
        qkv_done_tt = [None] * N_TT
        for tt in range(N_TT):
            g = bank_guard[3]
            bank_guard[3] = []
            for kc in range(KC):
                extra = wq_wait + W(v_b1, v_b2, v_b3) if (tt == 0 and kc == 0) else []
                v_mm = pe(lambda eng, tt=tt, kc=kc: eng.matmul(
                    pb[3][:, 0:384],
                    xn[:, kc * T_PASS + tt * 128: kc * T_PASS + (tt + 1) * 128],
                    wqkv[:, kc * 384:(kc + 1) * 384],
                    start=(kc == 0), stop=(kc == KC - 1)),
                    waits=W(("dve", xn_ready_tt[tt])) +
                          (g if kc == 0 else []) + extra,
                    inc=(kc == KC - 1))
            v_q = dve(lambda eng, tt=tt: eng.tensor_tensor(
                qkvf[:, tt * 384:(tt + 1) * 384], pb[3][:, 0:384],
                bias3[:], ALU.add),
                waits=W(("pe", v_mm)))
            v_q = dfence()
            bank_guard[3] = [("dve", v_q)]
            qkv_done_tt[tt] = v_q
        qkv_all_pe = S.val("pe")
        if DEBUG and p == 0:
            v_dwq = dma("pl", dbg["d_wqkv"], wqkv[:], waits=W(("pe", qkv_all_pe)))
        else:
            v_dwq = None

        for tt in range(N_TT):
            q0 = tt * 384
            v_km = tiny(lambda eng, q0=q0: eng.reduce_max(
                kab[:, 0:1].unsqueeze(2),
                qkvf[:, q0 + 128:q0 + 256].unsqueeze(1),
                AX.X, apply_absolute_value=True),
                waits=W(("dve", qkv_done_tt[tt]), ("pe", qkv_all_pe)))
            v_kn = tiny(lambda eng: eng.tensor_scalar(
                kab[:, 1:2], kab[:, 0:1], -1.0, None, ALU.mult),
                waits=W(("dve", v_km)))
            aw = W(("dve", qkv_done_tt[tt]), ("pe", qkv_all_pe),
                   v_dwq)
            v_qa = act(lambda eng, q0=q0: eng.activation(
                zn_t[:, 512:640], qkvf[:, q0:q0 + 128], AF.Abs), waits=aw)
            v_cn = dve(lambda eng: eng.tensor_scalar(
                zn_t[:, 640:768], zn_t[:, 512:640], kab[:, 1:2], None, ALU.mult),
                waits=W(("act", v_qa), ("dve", v_kn)))
            v_ev_prev = None
            v_zred_prev = None
            for db in range(16):
                v_sc = None
                for dd in range(8):
                    d = db * 8 + dd
                    v_sc = dve(lambda eng, q0=q0, d=d, dd=dd: eng.tensor_scalar(
                        sc_t[:, dd * 128:(dd + 1) * 128],
                        qkvf[:, q0 + 128:q0 + 256],
                        qkvf[:, q0 + d:q0 + d + 1],
                        zn_t[:, 640 + d:641 + d],
                        ALU.mult, ALU.add),
                        waits=W(("dve", v_cn), ("pool", v_ev_prev)))
                v_sc = dfence()
                v_e = act(lambda eng: eng.activation(et_t, sc_t, AF.Exp),
                          waits=W(("dve", v_sc)) +
                                (W(("dve", v_zred_prev)) if v_zred_prev else []))
                v_e = afence()
                v_ev = pool(lambda eng, q0=q0: eng.tensor_tensor(
                    sc_t.rearrange("p (a b) -> p a b", b=128),
                    et_t.rearrange("p (a b) -> p a b", b=128),
                    qkvf[:, q0 + 256:q0 + 384].unsqueeze(1).broadcast_to([128, 8, 128]),
                    ALU.mult),
                    waits=W(("act", v_e)))
                v_zred_prev = dve(lambda eng, db=db: eng.tensor_reduce(
                    zn_t[:, db * 8:(db + 1) * 8].unsqueeze(2),
                    et_t.rearrange("p (a b) -> p a b", b=128), AX.X, ALU.add),
                    waits=W(("act", v_e)))
                v_ev = plfence()
                v_nred = dve(lambda eng, db=db: eng.tensor_reduce(
                    zn_t[:, 128 + db * 8:128 + (db + 1) * 8].unsqueeze(2),
                    sc_t.rearrange("p (a b) -> p a b", b=128), AX.X, ALU.add),
                    waits=W(("pool", v_ev)))
                v_ev_prev = v_ev
            v_nred = dfence()
            act(lambda eng: eng.activation(zn_t[:, 256:384], zn_t[:, 0:128], AF.Ln),
                waits=W(("dve", v_nred)))
            v_zi = act(lambda eng: eng.activation(
                zn_t[:, 384:512], zn_t[:, 256:384], AF.Exp, scale=-1.0))
            dve(lambda eng, tt=tt: eng.tensor_tensor(
                o_attn[:, tt * 128:(tt + 1) * 128], zn_t[:, 128:256],
                zn_t[:, 384:512], ALU.mult),
                waits=W(("act", v_zi)))
        attn_done = S.val("dve")
        attn_pool_done = S.val("pool")

        it_g = 0
        h_done_it = [None] * NIT
        a_end_act = S.val("act")
        raw_cons = {}
        sbc_cons = {}
        raw_v = {}
        sbc_v = {}
        nraw_uses = NB * 16
        nsbc_uses = NB * 32

        def emit_raw(u):
            b2, rem = divmod(u, 16)
            grp, mat = divmod(rem, 2)
            c0b, bw, segs = GU_BLOCKS[b2]
            slot = u % 4
            prev = raw_cons.get(u - 4)
            assert prev is not None or u < 4
            wl = W(("dve", prev))
            if prev is None:
                wl = wl + W(("act", a_end_act))
            t_in = (gqt, uqt)[mat]
            ni_tot = bw // 2
            offp = GQT_OFF[(b2, grp)]
            srcap = t_in[offp:offp + 128 * 4 * ni_tot].rearrange(
                "(p n) -> p n", p=128)
            dst = raw_gu[:, slot * 1024: slot * 1024 + 4 * ni_tot]
            if p == 0:
                wl = wl + W(*v_gqt[(mat, b2, grp)])
            raw_v[u] = [dma("sp", dst, srcap, waits=wl)]

        def emit_sbc(u):
            b2, rem = divmod(u, 32)
            grp, mat = divmod(rem, 2)
            c0b, bw, segs = GU_BLOCKS[b2]
            slot = u % 4
            prev = sbc_cons.get(u - 4)
            assert prev is not None or u < 4
            wl = W(("dve", prev))
            if prev is None:
                wl = wl + W(("act", a_end_act))
            sp_dram = (sprep_g_d, sprep_u_d)[mat]
            srcap = sp_dram[grp * 2:(grp + 1) * 2, c0b:c0b + bw].partition_broadcast(128)
            dst = sbc_gu[:, slot * 1024: slot * 1024 + 2 * bw].rearrange(
                "p (c n) -> p c n", c=2)
            q = ("sp", "pl")[mat]
            sbc_v[u] = dma(q, dst, srcap, waits=wl)

        next_raw = [0]
        next_sbc = [0]

        def pump(ur_t, us_t):
            while next_raw[0] <= min(ur_t, nraw_uses - 1):
                emit_raw(next_raw[0])
                next_raw[0] += 1
            while next_sbc[0] <= min(us_t, nsbc_uses - 1):
                emit_sbc(next_sbc[0])
                next_sbc[0] += 1

        for b, (c0b, bw, segs) in enumerate(GU_BLOCKS):
            nit_b = bw // 128
            z_v = {}
            for mat in range(2):
                use = b * 2 + mat
                slot = use % 4
                zs_dram = (zs_g_d, zs_u_d)[mat]
                v = dma("sp", zring_gu[0:32, slot * 512: slot * 512 + bw],
                        zs_dram[:, c0b:c0b + bw],
                        waits=W(("pe", gu_z_free[slot])), sem="zg")
                z_v[mat] = (v, slot)
            for kc in range(KC):
                minfo = {}
                for mat in range(2):
                    ur = b * 16 + (kc // 4) * 2 + mat
                    us = b * 32 + (kc // 2) * 2 + mat
                    pump(ur + 2, us + 2)
                    use = b * 64 + kc * 2 + mat
                    wslot = use % 4
                    pslot = use % 2
                    rslot = ur % 4
                    sslot = us % 4
                    raw16 = raw_gu[:, rslot * 1024: rslot * 1024 + 2 * bw].bitcast(
                        U16).rearrange("p (c n) -> p c n", c=4)[:, kc % 4, :]
                    wl = W(*raw_v[ur]) + W(sbc_v[us],
                           ("pe", gu_wdq_free[wslot]))
                    off = 0
                    ioff = 0
                    for (j, w0, nw) in segs:
                        dve(lambda eng, j=j, nw=nw, off=off, ioff=ioff, r=raw16, ps=pslot:
                            eng.tensor_scalar(
                                plane_gu[:, ps * 512 + off: ps * 512 + off + nw],
                                r[:, ioff:ioff + nw],
                                4 * j, 15, ALU.logical_shift_right, ALU.bitwise_and),
                            waits=wl)
                        wl = ()
                        off += nw
                        ioff += nw
                    minfo[mat] = (wslot, sslot, pslot)
                if bw < 512 or len(segs) > 1:
                    dfence(2)
                for mat in range(2):
                    wslot, sslot, pslot = minfo[mat]
                    v_mul = dve(lambda eng, ws=wslot, ss=sslot, kc=kc, bw=bw, ps=pslot:
                                eng.tensor_tensor(
                                    wdq_gu[:, ws * 512: ws * 512 + bw],
                                    plane_gu[:, ps * 512: ps * 512 + bw],
                                    sbc_gu[:, ss * 1024 + (kc % 2) * bw:
                                           ss * 1024 + (kc % 2 + 1) * bw],
                                    ALU.mult))
                    minfo[mat] = wslot
                v_mulf = dfence()
                for mat in range(2):
                    wslot = minfo[mat]
                    if DEBUG and p == 0 and b == 21 and kc == 0 and mat == 0:
                        dve(lambda eng, ws=wslot: eng.tensor_copy(
                            dbg_w[:, 0:256], wdq_gu[:, ws * 512: ws * 512 + 256]),
                            waits=W(("dve", v_mulf)))
                    for it in range(nit_b):
                        bank = mat * 4 + it
                        wl2 = W(("dve", v_mulf))
                        if kc == 0:
                            wl2 = wl2 + bank_guard[bank] + W(("dve", xn_done))
                            bank_guard[bank] = []
                        v_mm = pe(lambda eng, bank=bank, ws=wslot, it=it, kc=kc:
                                  eng.matmul(
                                      pb[bank][:],
                                      wdq_gu[:, ws * 512 + it * 128:
                                             ws * 512 + (it + 1) * 128],
                                      xn[:, kc * T_PASS:(kc + 1) * T_PASS],
                                      start=(kc == 0), stop=False),
                                  waits=wl2, inc=(it == nit_b - 1))
                    gu_wdq_free[wslot] = v_mm
                    ur = b * 16 + (kc // 4) * 2 + mat
                    us = b * 32 + (kc // 2) * 2 + mat
                    if kc % 4 == 3:
                        raw_cons[ur] = S.val("dve")
                    if kc % 2 == 1:
                        sbc_cons[us] = S.val("dve")
            bank_done = {}
            for mat in range(2):
                v_z, zslot = z_v[mat]
                for it in range(nit_b):
                    bank = mat * 4 + it
                    v_mm = pe(lambda eng, bank=bank, zslot=zslot, it=it: eng.matmul(
                        pb[bank][:],
                        zring_gu[0:32, zslot * 512 + it * 128:
                                 zslot * 512 + (it + 1) * 128],
                        xs_bf[:], start=False, stop=True),
                        waits=W(("dve", v_xs), v_z), inc=True)
                    bank_done[bank] = v_mm
                gu_z_free[zslot] = v_mm

            for it in range(nit_b):
                itg = it_g + it
                gbank, ubank = it, 4 + it
                ss = itg % 2
                cg = act(lambda eng, gb=gbank, ss=ss: eng.activation(
                    psg[:, ss * 512:(ss + 1) * 512], pb[gb][:], AF.Copy),
                    waits=W(("pe", bank_done[gbank]), ("pool", psg_free[ss])) +
                          (snap_waits(setup_end, ["ad"])
                           if (p == 0 and it_g == 0 and it == 0) else []))
                cu = act(lambda eng, ub=ubank, ss=ss: eng.activation(
                    psg[:, 1024 + ss * 512:1024 + (ss + 1) * 512], pb[ub][:],
                    AF.Copy),
                    waits=W(("pe", bank_done[ubank])))
                bank_guard[gbank] = [("act", cg)]
                bank_guard[ubank] = [("act", cu)]
                act(lambda eng, ss=ss: eng.activation(
                    sil[:, ss * 512:(ss + 1) * 512],
                    psg[:, ss * 512:(ss + 1) * 512], AF.Abs),
                    waits=W(("pool", sil_free[ss])))
                act(lambda eng, ss=ss: eng.activation(
                    sil2[:, ss * 512:(ss + 1) * 512],
                    sil[:, ss * 512:(ss + 1) * 512], AF.Exp, scale=-1.0))
                act(lambda eng, ss=ss: eng.activation(
                    sil[:, ss * 512:(ss + 1) * 512],
                    sil2[:, ss * 512:(ss + 1) * 512], AF.Ln, bias=ones_c[:]),
                    waits=W(("dve", ones_ready)))
                act(lambda eng, ss=ss: eng.activation(
                    sil3[:, ss * 512:(ss + 1) * 512],
                    psg[:, ss * 512:(ss + 1) * 512], AF.Relu, scale=-1.0))
                act(lambda eng, ss=ss: eng.activation(
                    sil2[:, ss * 512:(ss + 1) * 512],
                    sil[:, ss * 512:(ss + 1) * 512], AF.Exp, scale=-1.0))
                act(lambda eng, ss=ss: eng.activation(
                    sil[:, ss * 512:(ss + 1) * 512],
                    sil3[:, ss * 512:(ss + 1) * 512], AF.Exp, scale=-1.0))
                v_silf = afence()
                pool(lambda eng, ss=ss: eng.tensor_tensor(
                    sil3[:, ss * 512:(ss + 1) * 512],
                    sil2[:, ss * 512:(ss + 1) * 512],
                    sil[:, ss * 512:(ss + 1) * 512], ALU.mult),
                    waits=W(("act", v_silf)))
                pool(lambda eng, ss=ss: eng.tensor_tensor(
                    sil[:, ss * 512:(ss + 1) * 512],
                    sil3[:, ss * 512:(ss + 1) * 512],
                    psg[:, ss * 512:(ss + 1) * 512], ALU.mult))
                if DEBUG and p == 0 and b == 21 and it == 1:
                    v_t0 = dve(lambda eng, zs=z_v[0][1]: eng.tensor_copy(
                        dbg_w[0:32, 768:896],
                        zring_gu[0:32, zs * 512 + 128: zs * 512 + 256]),
                        waits=W(("act", v_silf)))
                    v_t1 = dve(lambda eng: eng.tensor_copy(
                        dbg_g[:, 512:1024], pb[1][:]),
                        waits=W(("act", v_silf), ("dve", v_t0)))
                    dma("pl", dbg["d_wdq"], dbg_w[:], waits=W(("dve", v_t1)))
                    dma("pl", dbg["d_g0"], dbg_g[:], waits=W(("dve", v_t1)))
                pool(lambda eng, ss=ss, itg=itg: eng.tensor_tensor(
                    h[:, itg * 512:(itg + 1) * 512],
                    sil[:, ss * 512:(ss + 1) * 512],
                    psg[:, 1024 + ss * 512:1024 + (ss + 1) * 512], ALU.mult))
                v_h = plfence()
                sil_free[ss] = v_h
                psg_free[ss] = v_h
                h_done_it[itg] = v_h
            if DEBUG and p == 0 and b == 21:
                bank_guard[1].append(("dve", S.val("dve")))
            it_g += nit_b
        h_done = S.val("pool")
        if DEBUG and p == 0:
            dma("pl", dbg["d_h2"], h[:], waits=W(("pool", h_done)))

        for g_ in range(NIT):
            mslot = g_ % 4
            v_m = dma("ad", mring[:, mslot * NIT:(mslot + 1) * NIT],
                      m_in[g_ * 128:(g_ + 1) * 128, :],
                      waits=W(("pe", m_free[mslot])))
            gw = bank_guard[2] if g_ == 0 else []
            v_mm = pe(lambda eng, g_=g_, ms=mslot: eng.matmul(
                pb[2][0:NIT, :], mring[:, ms * NIT:(ms + 1) * NIT],
                h[:, g_ * 512:(g_ + 1) * 512],
                start=(g_ == 0), stop=(g_ == NIT - 1)),
                waits=W(v_m, ("pool", h_done_it[g_])) + gw,
                inc=True)
            m_free[mslot] = v_mm
        v_hs = dve(lambda eng: eng.tensor_copy(hs_bf[:], pb[2][0:NIT, :]),
                   waits=W(("pe", v_mm)))
        v_hs = dfence()
        bank_guard[2] = [("dve", v_hs)]

        nicb = (NIT + 3) // 4
        rawd_cons = {}
        sbcd_cons = {}
        rawd_v = {}
        sbcd_v = {}
        nd_uses = N_OS * nicb

        def emit_rawd(u):
            os2, icb = divmod(u, nicb)
            nic = min(4, NIT - icb * 4)
            rslot = u % 2
            prev = rawd_cons.get(u - 2)
            assert prev is not None or u < 2
            wl = (W(("dve", prev)) if prev is not None
                  else W(("pool", h_done)))
            src_ap = wdt[WDT_OFF[(os2, icb)]:WDT_OFF[(os2, icb)] + 128 * nic * 64
                         ].rearrange("(p n) -> p n", p=128)
            dst = raw_d[:, rslot * 256: rslot * 256 + nic * 64]
            if p == 0:
                wl = wl + W(v_wdt[(os2, icb)])
            rawd_v[u] = [dma("ad", dst, src_ap, waits=wl)]

        def emit_sbcd(u):
            os2, icb = divmod(u, nicb)
            nic = min(4, NIT - icb * 4)
            sslot = u % 2
            prev = sbcd_cons.get(u - 2)
            assert prev is not None or u < 2
            wl = (W(("dve", prev)) if prev is not None
                  else W(("pool", h_done)))
            src_ap = sdt[SDT_OFF[(os2, icb)]:SDT_OFF[(os2, icb)] + 128 * nic * 512
                         ].rearrange("(p n) -> p n", p=128)
            dst = sbc_d[:, sslot * 2048: sslot * 2048 + nic * 512]
            if p == 0:
                wl = wl + W(v_sdt[(os2, icb)])
            sbcd_v[u] = [dma("sp", dst, src_ap, waits=wl)]

        next_rd = [0]

        def pump_d(u_t):
            while next_rd[0] <= min(u_t, nd_uses - 1):
                emit_rawd(next_rd[0])
                emit_sbcd(next_rd[0])
                next_rd[0] += 1

        for os_ in range(N_OS):
            dnb = [(os_ % 2) * 4 + k for k in range(4)]
            zslot = os_ % 2
            v_zr = dma("sp", zring_d[0:NIT, zslot * 512:(zslot + 1) * 512],
                       zs_d_d[:, os_ * 512:(os_ + 1) * 512],
                       waits=W(("pe", dn_z_free[zslot]),
                               ("pool", h_done if os_ == 0 else 0)), sem="zd")
            for ic in range(NIT):
                icb, idx = ic // 4, ic % 4
                u = os_ * nicb + icb
                pump_d(u + 1)
                rslot = u % 2
                sslot = u % 2
                wslot = ic % 3
                raw16 = raw_d[:, rslot * 256 + idx * 64:
                              rslot * 256 + (idx + 1) * 64].bitcast(U16)
                wl = (W(*rawd_v[u]) + W(*sbcd_v[u]) +
                      W(("pe", dn_wdq_free[wslot])))
                for j in range(4):
                    dve(lambda eng, j=j, r=raw16: eng.tensor_scalar(
                        plane_dn[:, j * 128:(j + 1) * 128], r,
                        4 * j, 15, ALU.logical_shift_right, ALU.bitwise_and),
                        waits=wl)
                    wl = ()
                dfence(3)
                v_mul = dve(lambda eng, ws=wslot, ss=sslot, idx=idx:
                            eng.tensor_tensor(
                                wdq_dn[:, ws * 512:(ws + 1) * 512], plane_dn[:],
                                sbc_d[:, ss * 2048 + idx * 512:
                                      ss * 2048 + (idx + 1) * 512],
                                ALU.mult))
                v_mul = dfence()
                for tt in range(N_TT):
                    bank = dnb[tt]
                    wl2 = W(("dve", v_mul))
                    if ic == 0:
                        wl2 = wl2 + bank_guard[bank] + W(("pool", h_done))
                        bank_guard[bank] = []
                    v_mm = pe(lambda eng, bank=bank, ws=wslot, ic=ic, tt=tt:
                              eng.matmul(
                                  pb[bank][:],
                                  h[:, ic * 512 + tt * 128: ic * 512 + (tt + 1) * 128],
                                  wdq_dn[:, ws * 512:(ws + 1) * 512],
                                  start=(ic == 0), stop=False),
                              waits=wl2, inc=(tt == N_TT - 1))
                dn_wdq_free[wslot] = v_mm
                if idx == 3 or ic == NIT - 1:
                    rawd_cons[u] = S.val("dve")
                    sbcd_cons[u] = S.val("dve")
            for tt in range(N_TT):
                bank = dnb[tt]
                v_mm = pe(lambda eng, bank=bank, tt=tt, zs=zslot: eng.matmul(
                    pb[bank][:],
                    hs_bf[:, tt * 128:(tt + 1) * 128],
                    zring_d[0:NIT, zs * 512:(zs + 1) * 512],
                    start=False, stop=True),
                    waits=W(("dve", v_hs), v_zr), inc=True)
                if tt == N_TT - 1:
                    dn_z_free[zslot] = v_mm
                r0 = t0 + tt * 128
                u2 = (os_ * N_TT + tt) % 2
                v_xl = dma("ad", xsl[:, u2 * 512:(u2 + 1) * 512],
                           x_in[r0:r0 + 128, os_ * 512:(os_ + 1) * 512],
                           waits=W(("pool", xsl_free[u2])))
                cd = act(lambda eng, bank=bank, u2=u2: eng.activation(
                    psg[:, u2 * 512:(u2 + 1) * 512], pb[bank][:], AF.Copy),
                    waits=W(("pe", v_mm), ("pool", psg_free[u2])))
                bank_guard[bank] = [("act", cd)]
                act(lambda eng, u2=u2: eng.activation(
                    sil[:, u2 * 512:(u2 + 1) * 512],
                    psg[:, u2 * 512:(u2 + 1) * 512], AF.Abs),
                    waits=W(("pool", sil_free[u2])))
                act(lambda eng, u2=u2: eng.activation(
                    sil2[:, u2 * 512:(u2 + 1) * 512],
                    sil[:, u2 * 512:(u2 + 1) * 512], AF.Exp, scale=-1.0))
                act(lambda eng, u2=u2: eng.activation(
                    sil[:, u2 * 512:(u2 + 1) * 512],
                    sil2[:, u2 * 512:(u2 + 1) * 512], AF.Ln, bias=ones_c[:]),
                    waits=W(("dve", ones_ready)))
                act(lambda eng, u2=u2: eng.activation(
                    sil3[:, u2 * 512:(u2 + 1) * 512],
                    psg[:, u2 * 512:(u2 + 1) * 512], AF.Relu, scale=-1.0))
                act(lambda eng, u2=u2: eng.activation(
                    sil2[:, u2 * 512:(u2 + 1) * 512],
                    sil[:, u2 * 512:(u2 + 1) * 512], AF.Exp, scale=-1.0))
                act(lambda eng, u2=u2: eng.activation(
                    sil[:, u2 * 512:(u2 + 1) * 512],
                    sil3[:, u2 * 512:(u2 + 1) * 512], AF.Exp, scale=-1.0))
                v_silf2 = afence()
                pool(lambda eng, u2=u2: eng.tensor_tensor(
                    sil3[:, u2 * 512:(u2 + 1) * 512],
                    sil2[:, u2 * 512:(u2 + 1) * 512],
                    sil[:, u2 * 512:(u2 + 1) * 512], ALU.mult),
                    waits=W(("act", v_silf2)))
                pool(lambda eng, u2=u2: eng.tensor_tensor(
                    sil2[:, u2 * 512:(u2 + 1) * 512],
                    sil3[:, u2 * 512:(u2 + 1) * 512],
                    psg[:, u2 * 512:(u2 + 1) * 512], ALU.mult))
                v_sx = dve(lambda eng, tt=tt, os_=os_, u2=u2: eng.tensor_copy(
                    sexp[:, u2 * 128:(u2 + 1) * 128].rearrange(
                        "p (d m) -> p d m", m=8),
                    o_attn[:, tt * 128 + os_ * 16: tt * 128 + (os_ + 1) * 16
                           ].unsqueeze(2).broadcast_to([128, 16, 8])),
                    waits=W(("pool", sexp_free[u2]), ("dve", attn_done)))
                v_sx = dfence()
                v_sg = pool(lambda eng, u2=u2: eng.tensor_tensor(
                    sgb[:, u2 * 512:(u2 + 1) * 512].rearrange(
                        "p (j dm) -> p j dm", j=4),
                    sil2[:, u2 * 512:(u2 + 1) * 512].rearrange(
                        "p (j dm) -> p j dm", j=4),
                    sexp[:, u2 * 128:(u2 + 1) * 128].unsqueeze(1).broadcast_to(
                        [128, 4, 128]),
                    ALU.mult),
                    waits=W(("dve", v_sx), ("pool", sgb_free[u2])))
                sil_free[u2] = v_sg
                psg_free[u2] = v_sg
                sexp_free[u2] = v_sg
                v_out = pool(lambda eng, u2=u2: eng.tensor_tensor(
                    osl[:, u2 * 512:(u2 + 1) * 512].rearrange(
                        "p (m j) -> p m j", j=4),
                    sgb[:, u2 * 512:(u2 + 1) * 512].rearrange(
                        "p (j m) -> p m j", j=4),
                    xsl[:, u2 * 512:(u2 + 1) * 512].rearrange(
                        "p (m j) -> p m j", j=4),
                    ALU.add),
                    waits=W(v_xl, osl_free[u2]))
                v_out = plfence()
                xsl_free[u2] = v_out
                sgb_free[u2] = v_out
                v_st = dma("pl", out_t[r0:r0 + 128, os_ * 512:(os_ + 1) * 512],
                           osl[:, u2 * 512:(u2 + 1) * 512],
                           waits=W(("pool", v_out)))
                osl_free[u2] = v_st
        if DEBUG and p == 0:
            dma("pl", dbg["d_xn"], xn[:], waits=W(("dve", S.val("dve"))))
            dma("pl", dbg["d_qkvf"], qkvf[:], waits=W(("dve", S.val("dve"))))
            dma("pl", dbg["d_oattn"], o_attn[:], waits=W(("dve", attn_done)))
            dma("pl", dbg["d_stat"], stat[:], waits=W(("dve", S.val("dve"))))
            dma("pl", dbg["d_xs"], xs_bf[:], waits=W(("dve", S.val("dve"))))
            dma("pl", dbg["d_hs"], hs_bf[:], waits=W(("dve", S.val("dve"))))
            dma("pl", dbg["d_h"], h[:], waits=W(("dve", S.val("dve"))))
        prev_pass_end = S.snap()

    S.emit("sp", lambda eng: eng.nop(),
           waits=snap_waits(S.snap(), ["pl", "ad", "sp", "sd", "sd2"]))

    with nc.Block() as block:
        def make_runner(name):
            dma_sems = set(DMA_SEM_NAMES)

            def run(eng):
                last = {}
                for (fn, waits, inc, inc_n) in S.ev[name]:
                    for (sn, val) in waits:
                        if last.get(sn, 0) >= val:
                            continue
                        eng.wait_ge(sems[sn], val * DMAW if sn in dma_sems else val)
                        last[sn] = val
                    ins = fn(eng)
                    if inc is not None:
                        ins.then_inc(sems[inc],
                                     DMAW if inc in dma_sems else inc_n)
            return run

        block.sync(make_runner("sp"))
        block.tensor(make_runner("pe"))
        block.vector(make_runner("dve"))
        block.scalar(make_runner("act"))
        block.gpsimd(make_runner("pool"))

    ctx.close()
    return nc


_CACHE = {}


def _consts():
    ident = np.eye(128, dtype=ml_dtypes.bfloat16)
    oneh = np.zeros((128, 32 * 32), dtype=ml_dtypes.bfloat16)
    for kc in range(32):
        oneh[:, kc * 32 + kc] = 1.0
    m = np.zeros((I_DIM, NIT), dtype=ml_dtypes.bfloat16)
    for r in range(I_DIM):
        m[r, _cp_nat(r) // 128] = 1.0
    return ident, oneh, m


def kernel(L0_attn_x, L0_attn_Wq, L0_attn_bq, L0_attn_Wk, L0_attn_bk,
           L0_attn_Wv, L0_attn_bv,
           L0_ffn_gate_qweight, L0_ffn_gate_scales, L0_ffn_gate_qzeros,
           L0_ffn_up_qweight, L0_ffn_up_scales, L0_ffn_up_qzeros,
           L0_ffn_down_qweight, L0_ffn_down_scales, L0_ffn_down_qzeros):
    x = np.ascontiguousarray(np.asarray(L0_attn_x, dtype=np.float32)).reshape(8192, D)
    wq = np.ascontiguousarray(np.asarray(L0_attn_Wq, dtype=np.float32)).reshape(D, D)
    wk = np.ascontiguousarray(np.asarray(L0_attn_Wk, dtype=np.float32)).reshape(D, 128)
    wv = np.ascontiguousarray(np.asarray(L0_attn_Wv, dtype=np.float32)).reshape(D, 128)
    bq = np.ascontiguousarray(np.asarray(L0_attn_bq, dtype=np.float32)).reshape(128, 32)
    bk = np.ascontiguousarray(np.asarray(L0_attn_bk, dtype=np.float32)).reshape(1, 128)
    bv = np.ascontiguousarray(np.asarray(L0_attn_bv, dtype=np.float32)).reshape(1, 128)
    gqw = np.ascontiguousarray(np.asarray(L0_ffn_gate_qweight, dtype=np.int32))
    gsc = np.ascontiguousarray(np.asarray(L0_ffn_gate_scales, dtype=np.float32))
    gqz = np.ascontiguousarray(np.asarray(L0_ffn_gate_qzeros, dtype=np.int32))
    uqw = np.ascontiguousarray(np.asarray(L0_ffn_up_qweight, dtype=np.int32))
    usc = np.ascontiguousarray(np.asarray(L0_ffn_up_scales, dtype=np.float32))
    uqz = np.ascontiguousarray(np.asarray(L0_ffn_up_qzeros, dtype=np.int32))
    dqw = np.ascontiguousarray(np.asarray(L0_ffn_down_qweight, dtype=np.int32))
    dsc = np.ascontiguousarray(np.asarray(L0_ffn_down_scales, dtype=np.float32))
    dqz = np.ascontiguousarray(np.asarray(L0_ffn_down_qzeros, dtype=np.int32))

    ident, oneh, m = _consts()
    if "nc" not in _CACHE:
        _CACHE["nc"] = build_nc()
    nc = _CACHE["nc"]
    in_maps = []
    for c in range(N_CORES):
        in_maps.append(dict(
            x=x[c * T_CORE:(c + 1) * T_CORE], wq=wq, wk=wk, wv=wv,
            bq=bq, bk=bk, bv=bv,
            g_qw=gqw, g_sc=gsc, g_qz=gqz,
            u_qw=uqw, u_sc=usc, u_qz=uqz,
            d_qw=dqw, d_sc=dsc, d_qz=dqz,
            ident=ident, onehot=oneh, mmat=m,
        ))
    res = run_bass_kernel_spmd(nc, in_maps, core_ids=list(range(N_CORES)))
    out = np.concatenate([res.results[c]["out"] for c in range(N_CORES)], axis=0)
    return out.reshape(4, 2048, D)


# version 18
# speedup vs baseline: 1.4439x; 1.4439x over previous
import contextlib
import numpy as np
import ml_dtypes
import concourse.bass as bass
import concourse.mybir as mybir
from concourse.bass_utils import run_bass_kernel_spmd

F32 = mybir.dt.float32
BF16 = mybir.dt.bfloat16
I32 = mybir.dt.int32
U16 = mybir.dt.uint16
AF = mybir.ActivationFunctionType
ALU = mybir.AluOpType
AX = mybir.AxisListType

N_CORES = 8
T_CORE = 1024
T_PASS = 512
N_PASS = 2
N_TT = 4
D = 4096
KC = 32
I_DIM = 11008
NIT = 86
O_DIM = 4096
N_OS = 8
PLANE = 2752
GAMMA = 1.0 / np.sqrt(128.0)
EPS = 1e-5
DMAW = 16

ENGINES = ("sp", "pe", "dve", "act", "pool")
DEBUG = False


def _gu_blocks():
    blocks = []
    c = 0
    while c < I_DIM:
        bw = min(512, I_DIM - c)
        segs = []
        cc = c
        while cc < c + bw:
            j = cc // PLANE
            w0 = cc - j * PLANE
            nw = min(PLANE - w0, c + bw - cc)
            segs.append((j, w0, nw))
            cc += nw
        blocks.append((c, bw, segs))
        c += bw
    return blocks


GU_BLOCKS = _gu_blocks()
NB = len(GU_BLOCKS)


def _it_segs(it):
    segs = []
    c = 128 * it
    end = c + 128
    while c < end:
        j = c // PLANE
        w0 = c - j * PLANE
        nw = min(PLANE - w0, end - c)
        segs.append((j, w0, nw))
        c += nw
    return segs


IT_SEGS = [_it_segs(it) for it in range(NIT)]


def _cp_nat(cp):
    j = cp // PLANE
    return 4 * (cp - j * PLANE) + j


DMA_LANES = {"sp": 6, "ad": 6, "pl": 6, "sd": 3, "sd2": 3,
             "zg": 2, "zd": 2, "rp": 2, "su": 2}
DMA_SEM_NAMES = [b + str(i) for b, n in DMA_LANES.items() for i in range(n)]
COMPUTE_SEMS = ["pe", "dve", "act", "pool"]


class Sched:
    def __init__(self):
        self.ev = {e: [] for e in ENGINES}
        self.cnt = {n: 0 for n in DMA_SEM_NAMES + COMPUTE_SEMS}
        self.rr = {b: 0 for b in DMA_LANES}

    def emit(self, eng, fn, waits=(), inc=None, inc_n=1):
        if inc is not None:
            self.cnt[inc] += inc_n
        self.ev[eng].append((fn, tuple(waits), inc, inc_n))
        return self.cnt[inc] if inc is not None else None

    def val(self, sem):
        return self.cnt[sem]

    def snap(self):
        return dict(self.cnt)


def W(*pairs):
    out = []
    for p in pairs:
        if p is None:
            continue
        n, v = p
        if v:
            out.append((n, v))
    return out


def snap_waits(snap, bases):
    return [(b + str(i), snap[b + str(i)])
            for b in bases for i in range(DMA_LANES[b])
            if snap[b + str(i)]]


def build_nc():
    nc = bass.Bass("TRN2", target_bir_lowering=False, debug=False)

    def din(name, shape, dt):
        return nc.dram_tensor(name, shape, dt, kind="ExternalInput").ap()

    x_in = din("x", [T_CORE, D], F32)
    wq_in = din("wq", [D, D], F32)
    wk_in = din("wk", [D, 128], F32)
    wv_in = din("wv", [D, 128], F32)
    bq_in = din("bq", [128, 32], F32)
    bk_in = din("bk", [1, 128], F32)
    bv_in = din("bv", [1, 128], F32)
    gq_in = din("g_qw", [D, 1376], I32)
    gs_in = din("g_sc", [32, I_DIM], F32)
    gz_in = din("g_qz", [32, 1376], I32)
    uq_in = din("u_qw", [D, 1376], I32)
    us_in = din("u_sc", [32, I_DIM], F32)
    uz_in = din("u_qz", [32, 1376], I32)
    dq_in = din("d_qw", [I_DIM, 512], I32)
    ds_in = din("d_sc", [NIT, O_DIM], F32)
    dz_in = din("d_qz", [NIT, 512], I32)
    ident_in = din("ident", [128, 128], BF16)
    oneh_in = din("onehot", [128, 32 * 32], BF16)
    m_in = din("mmat", [I_DIM, NIT], BF16)

    out_t = nc.dram_tensor("out", [T_CORE, D], F32, kind="ExternalOutput").ap()
    if DEBUG:
        dbg = {
            "d_xn": nc.dram_tensor("d_xn", [128, KC * T_PASS], F32, kind="ExternalOutput").ap(),
            "d_wqkv": nc.dram_tensor("d_wqkv", [128, KC * 384], F32, kind="ExternalOutput").ap(),
            "d_qkvf": nc.dram_tensor("d_qkvf", [128, N_TT * 384], F32, kind="ExternalOutput").ap(),
            "d_oattn": nc.dram_tensor("d_oattn", [128, N_TT * 128], F32, kind="ExternalOutput").ap(),
            "d_stat": nc.dram_tensor("d_stat", [128, 64], F32, kind="ExternalOutput").ap(),
            "d_xs": nc.dram_tensor("d_xs", [32, T_PASS], F32, kind="ExternalOutput").ap(),
            "d_hs": nc.dram_tensor("d_hs", [NIT, T_PASS], F32, kind="ExternalOutput").ap(),
            "d_h": nc.dram_tensor("d_h", [128, NIT * T_PASS], F32, kind="ExternalOutput").ap(),
            "d_wdq": nc.dram_tensor("d_wdqg", [128, 2 * 512], BF16, kind="ExternalOutput").ap(),
            "d_h2": nc.dram_tensor("d_h2", [128, NIT * T_PASS], BF16, kind="ExternalOutput").ap(),
            "d_g0": nc.dram_tensor("d_g0", [128, 2 * 512], BF16, kind="ExternalOutput").ap(),
        }

    def dscr(name, shape, dt):
        return nc.dram_tensor(name, shape, dt, kind="Internal").ap()

    sprep_g_d = dscr("sprep_g", [32, I_DIM], BF16)
    sprep_u_d = dscr("sprep_u", [32, I_DIM], BF16)
    sprep_d_d = dscr("sprep_d", [NIT, O_DIM], BF16)
    zs_g_d = dscr("zsneg_g", [32, I_DIM], BF16)
    zs_u_d = dscr("zsneg_u", [32, I_DIM], BF16)
    zs_d_d = dscr("zsneg_d", [NIT, O_DIM], BF16)
    wqkv_d = dscr("wqkv_scr", [128, KC * 384], BF16)
    sd_exp = dscr("sd_exp", [NIT * 128, O_DIM], BF16)
    wd_perm = dscr("wd_perm", [I_DIM, 512], I32)
    bqs_d = dscr("bqs_scr", [128, 1], F32)

    ctx = contextlib.ExitStack()

    def sb(name, shape, dt):
        return ctx.enter_context(nc.sbuf_tensor(name, shape, dt))

    xn = sb("xn", [128, KC * T_PASS], BF16)
    h = sb("h", [128, NIT * T_PASS], BF16)
    ident_s = sb("ident_s", [128, 128], BF16)
    oneh_s = sb("oneh_s", [128, 32 * 32], BF16)
    bias3 = sb("bias3", [128, 384], F32)
    xs_bf = sb("xs_bf", [32, T_PASS], BF16)
    hs_bf = sb("hs_bf", [NIT, T_PASS], BF16)
    o_attn = sb("o_attn", [128, N_TT * 128], F32)
    qkvf = sb("qkvf", [128, N_TT * 384], F32)
    stat = sb("stat", [128, 64], F32)
    mring = sb("mring", [128, 4 * NIT], BF16)
    kab = sb("kab", [128, 4], F32)
    sexp = sb("sexp", [128, 2 * 128], F32)
    zring_gu = sb("zring_gu", [32, 4 * 512], BF16)
    zring_d = sb("zring_d", [NIT, 2 * 512], BF16)
    wdq_gu = sb("wdq_gu", [128, 4 * 512], BF16)
    wdq_dn = sb("wdq_dn", [128, 3 * 512], BF16)
    plane_gu = sb("plane_gu", [128, 2 * 512], U16)
    plane_dn = sb("plane_dn", [128, 512], U16)

    UA = sb("UA", [128, 8192], I32)
    UAf = UA[:].bitcast(F32)
    UAb = UA[:].bitcast(BF16)
    UAu = UA[:].bitcast(U16)
    xa = UAf[:, 0:2048]
    xb = UAf[:, 2048:4096]
    junk = UAf[:, 4096:6144]
    xn_tm = UAb[:, 12288:16384]
    raw_gu = UA[:][:, 0:4096]
    sbc_gu = UAb[:, 8192:12288]
    raw_d = UA[:][:, 0:512]
    sbc_d = UAb[:, 1024:5120]
    xsl = UAf[:, 2560:3584]
    osl = UAf[:, 3584:4608]
    sgb = UAf[:, 4608:5632]
    sil = UAf[:, 6144:7168]
    dbg_w = UAb[:, 14336:15360]
    dbg_g = UAb[:, 15360:16384]
    st_sc = UAf[:, 0:1376]
    st_sc2 = UAf[:, 1376:2752]
    st_qz = UA[:][:, 768:2144]
    st_zp = UAu[:, 4288:7040]
    st_zc = UAb[:, 7040:9792]

    UB = sb("UB", [128, 6144], I32)
    UBb = UB[:].bitcast(BF16)
    UBf = UB[:].bitcast(F32)
    sprep_sb = UBb[:, 0:I_DIM]
    wqkv = UBb[:, 0:KC * 384]
    sc_t = UBf[:, 0:1024]
    et_t = UBf[:, 1024:2048]
    zn_t = UBf[:, 2048:2816]

    pb = [ctx.enter_context(nc.psum_tensor(f"pb{i}", [128, 512], F32)) for i in range(8)]

    S = Sched()
    sems = {n: ctx.enter_context(nc.semaphore("sem_" + n))
            for n in DMA_SEM_NAMES + COMPUTE_SEMS}

    def dma(queue, dst, src, waits=(), sem=None):
        eng_name = {"sp": "sp", "ad": "act", "pl": "pool"}[queue]
        base = sem or queue
        lane = base + str(S.rr[base] % DMA_LANES[base])
        S.rr[base] += 1
        prev = S.cnt[lane]
        wl = list(waits)
        if prev:
            wl.append((lane, prev))

        def fn(eng, dst=dst, src=src):
            return eng.dma_start(out=dst, in_=src)
        return (lane, S.emit(eng_name, fn, waits=wl, inc=lane, inc_n=1))

    def pe(fn, waits=(), inc=False):
        return S.emit("pe", fn, waits=waits, inc="pe" if inc else None)

    def dve(fn, waits=(), inc=True):
        return S.emit("dve", fn, waits=waits, inc="dve" if inc else None)

    def act(fn, waits=(), inc=True):
        return S.emit("act", fn, waits=waits, inc="act" if inc else None)

    def pool(fn, waits=(), inc=True):
        return S.emit("pool", fn, waits=waits, inc="pool" if inc else None)

    def dfence(n=2):
        for _ in range(n - 1):
            S.emit("dve", lambda eng: eng.engine_nop(), inc="dve")
        return S.emit("dve", lambda eng: eng.engine_nop(), inc="dve")

    def afence(n=2):
        for _ in range(n - 1):
            S.emit("act", lambda eng: eng.mul(kab[0:1, 2:3], kab[0:1, 2:3], 1.0),
                   inc="act")
        return S.emit("act", lambda eng: eng.mul(kab[0:1, 2:3], kab[0:1, 2:3], 1.0),
                      inc="act")

    def plfence(n=2):
        for _ in range(n - 1):
            S.emit("pool", lambda eng: eng.engine_nop(), inc="pool")
        return S.emit("pool", lambda eng: eng.engine_nop(), inc="pool")

    def tiny(fn, waits=()):
        S.emit("dve", fn, waits=waits, inc="dve")
        S.emit("dve", lambda eng: eng.engine_nop(), inc="dve")
        return S.emit("dve", lambda eng: eng.engine_nop(), inc="dve")

    for gi_, (sc_in, qz_inp, sp_dram, zs_dram) in enumerate(
            [(gs_in, gz_in, sprep_g_d, zs_g_d), (us_in, uz_in, sprep_u_d, zs_u_d)]):
        v_sp = None
        ad_bar = S.snap()
        dve_bar = S.val("dve")
        for c4 in range(4):
            v_l = dma("sp", st_sc[0:32, :],
                      sc_in[:, c4 * 2752: c4 * 2752 + 1376],
                      waits=W(("dve", v_sp if v_sp else dve_bar)) +
                            (snap_waits(ad_bar, ["ad"]) if c4 == 0 else []), sem="su")
            v_l2 = dma("sp", st_sc2[0:32, :],
                       sc_in[:, c4 * 2752 + 1376:(c4 + 1) * 2752],
                       waits=W(("dve", v_sp if v_sp else dve_bar)), sem="su")
            for j in range(4):
                src1 = st_sc[0:32, :].rearrange("p (w f) -> p w f", f=4)[:, :, j]
                src2 = st_sc2[0:32, :].rearrange("p (w f) -> p w f", f=4)[:, :, j]
                d0 = j * PLANE + c4 * 688
                dve(lambda eng, s=src1, d0=d0: eng.tensor_copy(
                    sprep_sb[0:32, d0:d0 + 344], s),
                    waits=W(v_l))
                v_sp = dve(lambda eng, s=src2, d0=d0: eng.tensor_copy(
                    sprep_sb[0:32, d0 + 344:d0 + 688], s),
                    waits=W(v_l2))
        v_spw = dma("ad", sp_dram, sprep_sb[0:32, :], waits=W(("dve", v_sp)))
        v_qz = dma("sp", st_qz[0:32, :], qz_inp, waits=W(("dve", v_sp)), sem="su")
        qz16 = st_qz[0:32, :].bitcast(U16)
        v_zsw = None
        for j in range(4):
            dve(lambda eng, q=qz16, j=j: eng.tensor_scalar(
                st_zp[0:32, :], q, 4 * j, 15,
                ALU.logical_shift_right, ALU.bitwise_and),
                waits=W(v_qz, v_zsw))
            v_zs = dve(lambda eng, j=j: eng.scalar_tensor_tensor(
                st_zc[0:32, :], st_zp[0:32, :], -1.0,
                sprep_sb[0:32, j * PLANE:(j + 1) * PLANE], ALU.mult, ALU.mult))
            v_zsw = dma("ad", zs_dram[:, j * PLANE:(j + 1) * PLANE],
                        st_zc[0:32, :], waits=W(("dve", v_zs)))

    v_dsp = None
    chunk_v = [None, None, None]
    for ck in range(3):
        n0 = ck * 1376
        nn = min(1376, O_DIM - n0)
        v_l = dma("sp", st_sc[0:NIT, 0:nn], ds_in[:, n0:n0 + nn],
                  waits=W(("dve", v_dsp), v_zsw), sem="su")
        for s_ in range(8):
            for j in range(4):
                mlo = max(0, -(-(n0 - 512 * s_ - j) // 4))
                mhi = min(128, (n0 + nn - 512 * s_ - j + 3) // 4)
                if mlo >= mhi:
                    continue
                w0 = 512 * s_ + 4 * mlo + j - n0
                srcv = st_sc[0:NIT, w0:w0 + 4 * (mhi - mlo - 1) + 1:4]
                d0 = 512 * s_ + 128 * j + mlo
                v_dsp = dve(lambda eng, s=srcv, d0=d0, n=mhi - mlo:
                            eng.tensor_copy(sprep_sb[0:NIT, d0:d0 + n], s),
                            waits=W(v_l))
    v_dsp = dfence()
    v_spdw = dma("ad", sprep_d_d, sprep_sb[0:NIT, 0:O_DIM], waits=W(("dve", v_dsp)))
    v_dqz = dma("sp", st_qz[0:NIT, 0:512], dz_in, waits=W(("dve", v_dsp)), sem="su")
    qzd16 = st_qz[0:NIT, 0:512].bitcast(U16)
    v_zdw = None
    for s_ in range(8):
        for j in range(4):
            dve(lambda eng, s_=s_, j=j: eng.tensor_scalar(
                st_zp[0:NIT, 0:128], qzd16[:, 128 * s_:128 * s_ + 128],
                4 * j, 15, ALU.logical_shift_right, ALU.bitwise_and),
                waits=W(v_dqz, v_zdw))
            dfence(3)
            v_zd = dve(lambda eng, s_=s_, j=j: eng.scalar_tensor_tensor(
                st_zc[0:NIT, 0:128], st_zp[0:NIT, 0:128], -1.0,
                sprep_sb[0:NIT, s_ * 512 + j * 128: s_ * 512 + (j + 1) * 128],
                ALU.mult, ALU.mult))
            v_zd = dfence()
            v_zdw = dma("ad", zs_d_d[:, s_ * 512 + j * 128: s_ * 512 + (j + 1) * 128],
                        st_zc[0:NIT, 0:128], waits=W(("dve", v_zd)))

    v_sdx = None
    for ic in range(NIT):
        poff = 0
        for (j, w0, nw) in IT_SEGS[ic]:
            pos = 0
            while pos < nw:
                gg = (4 * (w0 + pos) + j) // 128
                run = min(nw - pos, 32 - ((w0 + pos) % 32))
                srcap = sprep_d_d[gg:gg + 1, :].broadcast_to([run, O_DIM])
                dst = sd_exp[ic * 128 + poff + pos: ic * 128 + poff + pos + run, :]
                v_sdx = dma("sp", dst, srcap, waits=W(v_spdw), sem="su")
                pos += run
            poff += nw
    v_wdp = None
    for j in range(4):
        srcap = dq_in.rearrange("(w f) n -> w f n", f=4)[:, j, :]
        dst = wd_perm[j * PLANE:(j + 1) * PLANE, :]
        v_wdp = dma("sp", dst, srcap, sem="su")

    v_id = dma("sp", ident_s[:], ident_in)
    v_oh = dma("sp", oneh_s[:], oneh_in)

    v_red = None
    for kc in range(KC):
        wl = W(v_zdw) if kc == 0 else W(("dve", v_red))
        v_l = dma("ad", xa, wq_in[kc * 128:(kc + 1) * 128, 0:2048], waits=wl)
        v_l2 = dma("ad", xb, wq_in[kc * 128:(kc + 1) * 128, 2048:4096])
        v_l3 = dma("ad", junk[:, 0:128], wk_in[kc * 128:(kc + 1) * 128, :])
        v_l4 = dma("ad", junk[:, 128:256], wv_in[kc * 128:(kc + 1) * 128, :])
        tiny(lambda eng: eng.reduce_sum(
            junk[:, 512:576].unsqueeze(2),
            xa.rearrange("p (d f) -> p d f", f=32), AX.X),
            waits=W(v_l))
        v_r2 = tiny(lambda eng: eng.reduce_sum(
            junk[:, 576:640].unsqueeze(2),
            xb.rearrange("p (d f) -> p d f", f=32), AX.X),
            waits=W(v_l2))
        dve(lambda eng, kc=kc: eng.tensor_scalar(
            wqkv[:, kc * 384: kc * 384 + 128], junk[:, 512:640],
            GAMMA, None, ALU.mult), waits=W(("dve", v_r2)))
        v_red = dve(lambda eng, kc=kc: eng.tensor_copy(
            wqkv[:, kc * 384 + 128: kc * 384 + 384], junk[:, 0:256]),
            waits=W(v_l4))
    v_red = dfence()
    v_wqkv_w = dma("ad", wqkv_d, wqkv[:], waits=W(("dve", v_red)))

    v_bq = dma("ad", xa[:, 0:32], bq_in, waits=W(("dve", v_red)))
    v_br = tiny(lambda eng: eng.reduce_sum(xb[:, 0:1].unsqueeze(2),
                                           xa[:, 0:32].unsqueeze(1), AX.X),
                waits=W(v_bq))
    v_bqs = tiny(lambda eng: eng.tensor_scalar(
        xb[:, 1:2], xb[:, 0:1], GAMMA, None, ALU.mult),
        waits=W(("dve", v_br)))
    v_bqsw = dma("ad", bqs_d, xb[:, 1:2], waits=W(("dve", v_bqs)))
    v_b1 = dma("sp", bias3[:, 0:128],
               bqs_d.rearrange("p one -> one p").partition_broadcast(128).squeeze(1),
               waits=W(v_bqsw))
    v_b2 = dma("sp", bias3[:, 128:256], bk_in.partition_broadcast(128).squeeze(1))
    v_b3 = dma("sp", bias3[:, 256:384], bv_in.partition_broadcast(128).squeeze(1))
    setup_end = S.snap()

    bank_guard = [[] for _ in range(8)]
    gu_raw_free = [None] * 4
    gu_sbc_free = [None] * 4
    gu_wdq_free = [None] * 4
    gu_z_free = [None] * 4
    dn_raw_free = [None] * 2
    dn_sbc_free = [None] * 2
    dn_wdq_free = [None] * 3
    dn_z_free = [None] * 2
    m_free = [None] * 4
    sil_free = [None] * 2
    xsl_free = [None] * 2
    osl_free = [None] * 2
    sgb_free = [None] * 2
    sexp_free = [None] * 2
    prev_pass_end = None

    for p in range(N_PASS):
        t0 = p * T_PASS

        xn_ready_tt = [None] * N_TT
        half_free = [setup_end["act"], setup_end["act"]]
        tm_free = [None, None]
        for tt in range(N_TT):
            r0 = t0 + tt * 128
            half_loads = []
            v_sq = [None, None]
            for hf in range(2):
                buf = (xa, xb)[hf]
                wl = W(("act", half_free[hf]))
                if tt == 0:
                    if p == 0:
                        wl = wl + snap_waits(setup_end, ["sp", "su"]) + W(("dve", setup_end["dve"]))
                    else:
                        wl = wl + W(("dve", prev_pass_end["dve"])) + \
                            snap_waits(prev_pass_end, ["pl", "sd", "sd2"])
                v_x = dma("ad", buf,
                          x_in[r0:r0 + 128, hf * 2048:(hf + 1) * 2048], waits=wl)
                half_loads.append(v_x)
                dve(lambda eng, b=buf, tt=tt, hf=hf: eng.reduce_sum(
                    stat[:, tt * 16 + hf: tt * 16 + hf + 1].unsqueeze(2),
                    b.unsqueeze(1), AX.X),
                    waits=W(v_x))
                jw = W(v_x)
                if p == 0 and tt == 0:
                    jw = jw + snap_waits(setup_end, ["sp", "su"])
                v_sq[hf] = act(lambda eng, b=buf, tt=tt, hf=hf: eng.activation(
                    junk, b, AF.Square,
                    accum_out=stat[:, tt * 16 + 2 + hf: tt * 16 + 3 + hf]),
                    waits=jw)
            c0 = tt * 16
            v_c = tiny(lambda eng, c0=c0: eng.tensor_tensor(
                stat[:, c0 + 4:c0 + 5], stat[:, c0:c0 + 1],
                stat[:, c0 + 1:c0 + 2], ALU.add),
                waits=W(("act", v_sq[1])))
            v_c = tiny(lambda eng, c0=c0: eng.tensor_tensor(
                stat[:, c0 + 5:c0 + 6], stat[:, c0 + 2:c0 + 3],
                stat[:, c0 + 3:c0 + 4], ALU.add), waits=W(("dve", v_c)))
            v_c = tiny(lambda eng, c0=c0: eng.tensor_scalar(
                stat[:, c0 + 6:c0 + 7], stat[:, c0 + 4:c0 + 5],
                1.0 / D, None, ALU.mult), waits=W(("dve", v_c)))
            v_c = tiny(lambda eng, c0=c0: eng.tensor_scalar(
                stat[:, c0 + 7:c0 + 8], stat[:, c0 + 5:c0 + 6],
                1.0 / D, None, ALU.mult), waits=W(("dve", v_c)))
            v_c = tiny(lambda eng, c0=c0: eng.tensor_tensor(
                stat[:, c0 + 8:c0 + 9], stat[:, c0 + 6:c0 + 7],
                stat[:, c0 + 6:c0 + 7], ALU.mult), waits=W(("dve", v_c)))
            v_c = tiny(lambda eng, c0=c0: eng.tensor_tensor(
                stat[:, c0 + 9:c0 + 10], stat[:, c0 + 7:c0 + 8],
                stat[:, c0 + 8:c0 + 9], ALU.subtract), waits=W(("dve", v_c)))
            v_var = tiny(lambda eng, c0=c0: eng.tensor_scalar(
                stat[:, c0 + 10:c0 + 11], stat[:, c0 + 9:c0 + 10],
                EPS, None, ALU.add), waits=W(("dve", v_c)))
            act(lambda eng, c0=c0: eng.activation(
                stat[:, c0 + 11:c0 + 12], stat[:, c0 + 10:c0 + 11], AF.Ln),
                waits=W(("dve", v_var)))
            v_rstd = act(lambda eng, c0=c0: eng.activation(
                stat[:, c0 + 12:c0 + 13], stat[:, c0 + 11:c0 + 12],
                AF.Exp, scale=-0.5))
            v_nb = tiny(lambda eng, c0=c0: eng.scalar_tensor_tensor(
                stat[:, c0 + 13:c0 + 14], stat[:, c0 + 6:c0 + 7], -1.0,
                stat[:, c0 + 12:c0 + 13], ALU.mult, ALU.mult),
                waits=W(("act", v_rstd)))
            last_ev = None
            for hf in range(2):
                sl = hf
                wl = W(("dve", v_nb), half_loads[hf],
                       ("pe", tm_free[sl]))
                v_tm = act(lambda eng, b=(xa, xb)[hf], c0=c0, sl=sl: eng.activation(
                    xn_tm[:, sl * 2048:(sl + 1) * 2048], b, AF.Identity,
                    bias=stat[:, c0 + 13:c0 + 14], scale=stat[:, c0 + 12:c0 + 13]),
                    waits=wl)
                v_tm = afence()
                half_free[hf] = v_tm
                for k2 in range(16):
                    kc = hf * 16 + k2
                    bank = kc % 4
                    pview = pb[bank][:].bitcast(BF16)
                    g = bank_guard[bank]
                    bank_guard[bank] = []
                    v_mm = pe(lambda eng, pv=pview, k2=k2, sl=sl:
                              eng.transpose(pv[:, 0:128],
                                            xn_tm[:, sl * 2048 + k2 * 128:
                                                  sl * 2048 + (k2 + 1) * 128],
                                            ident_s[:]),
                              waits=W(("act", v_tm)) + g, inc=True)
                    v_ev = dve(lambda eng, pv=pview, kc=kc, tt=tt:
                               eng.tensor_copy(
                                   xn[:, kc * T_PASS + tt * 128:
                                      kc * T_PASS + (tt + 1) * 128],
                                   pv[:, 0:128]),
                               waits=W(("pe", v_mm)))
                    bank_guard[bank].append(("dve", v_ev))
                    last_ev = v_ev
                tm_free[sl] = S.val("pe")
            xn_ready_tt[tt] = dfence()
        xn_done = S.val("dve")

        for kc in range(KC):
            g = bank_guard[2] if kc == 0 else []
            ow = snap_waits(setup_end, ["sp"]) if (p == 0 and kc == 0) else []
            v_mm = pe(lambda eng, kc=kc: eng.matmul(
                pb[2][0:32, :], oneh_s[:, kc * 32:(kc + 1) * 32],
                xn[:, kc * T_PASS:(kc + 1) * T_PASS],
                start=(kc == 0), stop=(kc == KC - 1)),
                waits=W(("dve", xn_done)) + g + ow, inc=(kc == KC - 1))
        v_xs = dve(lambda eng: eng.tensor_copy(xs_bf[:], pb[2][0:32, :]),
                   waits=W(("pe", v_mm)))
        v_xs = dfence()
        bank_guard[2] = [("dve", v_xs)]

        if p > 0:
            v_wl = dma("ad", wqkv[:], wqkv_d,
                       waits=W(("dve", prev_pass_end["dve"]),
                               ("pool", prev_pass_end["pool"]),
                               ("act", prev_pass_end["act"])))
            wq_wait = W(v_wl)
        else:
            wq_wait = W(("dve", v_red))
        qkv_done_tt = [None] * N_TT
        for tt in range(N_TT):
            g = bank_guard[3]
            bank_guard[3] = []
            for kc in range(KC):
                extra = wq_wait + W(v_b1, v_b2, v_b3) if (tt == 0 and kc == 0) else []
                v_mm = pe(lambda eng, tt=tt, kc=kc: eng.matmul(
                    pb[3][:, 0:384],
                    xn[:, kc * T_PASS + tt * 128: kc * T_PASS + (tt + 1) * 128],
                    wqkv[:, kc * 384:(kc + 1) * 384],
                    start=(kc == 0), stop=(kc == KC - 1)),
                    waits=W(("dve", xn_ready_tt[tt])) +
                          (g if kc == 0 else []) + extra,
                    inc=(kc == KC - 1))
            v_q = dve(lambda eng, tt=tt: eng.tensor_tensor(
                qkvf[:, tt * 384:(tt + 1) * 384], pb[3][:, 0:384],
                bias3[:], ALU.add),
                waits=W(("pe", v_mm)))
            v_q = dfence()
            bank_guard[3] = [("dve", v_q)]
            qkv_done_tt[tt] = v_q
        qkv_all_pe = S.val("pe")
        if DEBUG and p == 0:
            v_dwq = dma("pl", dbg["d_wqkv"], wqkv[:], waits=W(("pe", qkv_all_pe)))
        else:
            v_dwq = None

        def d_phase():
          for tt in range(N_TT):
              q0 = tt * 384
              v_km = tiny(lambda eng, q0=q0: eng.reduce_max(
                  kab[:, 0:1].unsqueeze(2),
                  qkvf[:, q0 + 128:q0 + 256].unsqueeze(1),
                  AX.X, apply_absolute_value=True),
                  waits=W(("dve", qkv_done_tt[tt]), ("pe", qkv_all_pe)))
              v_kn = tiny(lambda eng: eng.tensor_scalar(
                  kab[:, 1:2], kab[:, 0:1], -1.0, None, ALU.mult),
                  waits=W(("dve", v_km)))
              aw = W(("dve", qkv_done_tt[tt]), ("pe", qkv_all_pe),
                     v_dwq)
              v_qa = act(lambda eng, q0=q0: eng.activation(
                  zn_t[:, 512:640], qkvf[:, q0:q0 + 128], AF.Abs), waits=aw)
              v_cn = dve(lambda eng: eng.tensor_scalar(
                  zn_t[:, 640:768], zn_t[:, 512:640], kab[:, 1:2], None, ALU.mult),
                  waits=W(("act", v_qa), ("dve", v_kn)))
              yield
              v_ev_prev = None
              v_zred_prev = None
              for db in range(16):
                  v_sc = None
                  for dd in range(8):
                      d = db * 8 + dd
                      v_sc = dve(lambda eng, q0=q0, d=d, dd=dd: eng.tensor_scalar(
                          sc_t[:, dd * 128:(dd + 1) * 128],
                          qkvf[:, q0 + 128:q0 + 256],
                          qkvf[:, q0 + d:q0 + d + 1],
                          zn_t[:, 640 + d:641 + d],
                          ALU.mult, ALU.add),
                          waits=W(("dve", v_cn), ("pool", v_ev_prev)))
                  v_sc = dfence()
                  v_e = act(lambda eng: eng.activation(et_t, sc_t, AF.Exp),
                            waits=W(("dve", v_sc)) +
                                  (W(("dve", v_zred_prev)) if v_zred_prev else []))
                  v_e = afence()
                  v_ev = pool(lambda eng, q0=q0: eng.tensor_tensor(
                      sc_t.rearrange("p (a b) -> p a b", b=128),
                      et_t.rearrange("p (a b) -> p a b", b=128),
                      qkvf[:, q0 + 256:q0 + 384].unsqueeze(1).broadcast_to([128, 8, 128]),
                      ALU.mult),
                      waits=W(("act", v_e)))
                  v_zred_prev = dve(lambda eng, db=db: eng.tensor_reduce(
                      zn_t[:, db * 8:(db + 1) * 8].unsqueeze(2),
                      et_t.rearrange("p (a b) -> p a b", b=128), AX.X, ALU.add),
                      waits=W(("act", v_e)))
                  v_ev = plfence()
                  v_nred = dve(lambda eng, db=db: eng.tensor_reduce(
                      zn_t[:, 128 + db * 8:128 + (db + 1) * 8].unsqueeze(2),
                      sc_t.rearrange("p (a b) -> p a b", b=128), AX.X, ALU.add),
                      waits=W(("pool", v_ev)))
                  v_ev_prev = v_ev
                  yield
              v_nred = dfence()
              act(lambda eng: eng.activation(zn_t[:, 256:384], zn_t[:, 0:128], AF.Ln),
                  waits=W(("dve", v_nred)))
              v_zi = act(lambda eng: eng.activation(
                  zn_t[:, 384:512], zn_t[:, 256:384], AF.Exp, scale=-1.0))
              dve(lambda eng, tt=tt: eng.tensor_tensor(
                  o_attn[:, tt * 128:(tt + 1) * 128], zn_t[:, 128:256],
                  zn_t[:, 384:512], ALU.mult),
                  waits=W(("act", v_zi)))
              yield
        dgen = d_phase()
        nd_chunks = N_TT * 18
        nd_done = [0]

        def pump_d_chunks(tgt):
            while nd_done[0] < tgt and next(dgen, "END") != "END":
                nd_done[0] += 1

        it_g = 0
        h_done_it = [None] * NIT
        a_end_act = S.val("act")
        raw_cons = {}
        sbc_cons = {}
        raw_v = {}
        sbc_v = {}
        nraw_uses = NB * 16
        nsbc_uses = NB * 32

        def emit_raw(u):
            b2, rem = divmod(u, 16)
            grp, mat = divmod(rem, 2)
            c0b, bw, segs = GU_BLOCKS[b2]
            slot = u % 4
            prev = raw_cons.get(u - 4)
            assert prev is not None or u < 4
            wl = W(("dve", prev))
            if prev is None:
                wl = wl + W(("act", a_end_act))
            q_inp = (gq_in, uq_in)[mat]
            vs = []
            off = 0
            for (j, w0, nw) in segs:
                i0, ni = w0 // 2, nw // 2
                srcap = q_inp.rearrange("(c p) n -> c p n", p=128)[
                    grp * 4:(grp + 1) * 4, :, i0:i0 + ni].transpose([1, 0, 2])
                dst = raw_gu[:, slot * 1024:(slot + 1) * 1024].rearrange(
                    "p (c n) -> p c n", c=4)[:, :, off:off + ni]
                vs.append(dma("sp", dst, srcap, waits=wl))
                wl = ()
                off += ni
            raw_v[u] = vs

        def emit_sbc(u):
            b2, rem = divmod(u, 32)
            grp, mat = divmod(rem, 2)
            c0b, bw, segs = GU_BLOCKS[b2]
            slot = u % 4
            prev = sbc_cons.get(u - 4)
            assert prev is not None or u < 4
            wl = W(("dve", prev))
            if prev is None:
                wl = wl + W(("act", a_end_act))
            sp_dram = (sprep_g_d, sprep_u_d)[mat]
            srcap = sp_dram[grp * 2:(grp + 1) * 2, c0b:c0b + bw].partition_broadcast(128)
            dst = sbc_gu[:, slot * 1024: slot * 1024 + 2 * bw].rearrange(
                "p (c n) -> p c n", c=2)
            q = ("sp", "pl")[mat]
            sbc_v[u] = dma(q, dst, srcap, waits=wl)

        next_raw = [0]
        next_sbc = [0]

        def pump(ur_t, us_t):
            while next_raw[0] <= min(ur_t, nraw_uses - 1):
                emit_raw(next_raw[0])
                next_raw[0] += 1
            while next_sbc[0] <= min(us_t, nsbc_uses - 1):
                emit_sbc(next_sbc[0])
                next_sbc[0] += 1

        for b, (c0b, bw, segs) in enumerate(GU_BLOCKS):
            nit_b = bw // 128
            z_v = {}
            for mat in range(2):
                use = b * 2 + mat
                slot = use % 4
                zs_dram = (zs_g_d, zs_u_d)[mat]
                v = dma("sp", zring_gu[0:32, slot * 512: slot * 512 + bw],
                        zs_dram[:, c0b:c0b + bw],
                        waits=W(("pe", gu_z_free[slot])), sem="zg")
                z_v[mat] = (v, slot)
            for kc in range(KC):
                minfo = {}
                for mat in range(2):
                    ur = b * 16 + (kc // 4) * 2 + mat
                    us = b * 32 + (kc // 2) * 2 + mat
                    pump(ur + 2, us + 2)
                    use = b * 64 + kc * 2 + mat
                    wslot = use % 4
                    pslot = use % 2
                    rslot = ur % 4
                    sslot = us % 4
                    raw16 = raw_gu[:, rslot * 1024:(rslot + 1) * 1024].bitcast(U16).rearrange(
                        "p (c n) -> p c n", c=4)[:, kc % 4, :]
                    wl = W(*raw_v[ur]) + W(sbc_v[us],
                           ("pe", gu_wdq_free[wslot]))
                    off = 0
                    ioff = 0
                    for (j, w0, nw) in segs:
                        dve(lambda eng, j=j, nw=nw, off=off, ioff=ioff, r=raw16, ps=pslot:
                            eng.tensor_scalar(
                                plane_gu[:, ps * 512 + off: ps * 512 + off + nw],
                                r[:, ioff:ioff + nw],
                                4 * j, 15, ALU.logical_shift_right, ALU.bitwise_and),
                            waits=wl)
                        wl = ()
                        off += nw
                        ioff += nw
                    minfo[mat] = (wslot, sslot, pslot)
                if bw < 512 or len(segs) > 1:
                    dfence(2)
                for mat in range(2):
                    wslot, sslot, pslot = minfo[mat]
                    v_mul = dve(lambda eng, ws=wslot, ss=sslot, kc=kc, bw=bw, ps=pslot:
                                eng.tensor_tensor(
                                    wdq_gu[:, ws * 512: ws * 512 + bw],
                                    plane_gu[:, ps * 512: ps * 512 + bw],
                                    sbc_gu[:, ss * 1024 + (kc % 2) * bw:
                                           ss * 1024 + (kc % 2 + 1) * bw],
                                    ALU.mult))
                    minfo[mat] = wslot
                v_mulf = dfence()
                for mat in range(2):
                    wslot = minfo[mat]
                    if DEBUG and p == 0 and b == 21 and kc == 0 and mat == 0:
                        dve(lambda eng, ws=wslot: eng.tensor_copy(
                            dbg_w[:, 0:256], wdq_gu[:, ws * 512: ws * 512 + 256]),
                            waits=W(("dve", v_mulf)))
                    for it in range(nit_b):
                        bank = mat * 4 + it
                        wl2 = W(("dve", v_mulf))
                        if kc == 0:
                            wl2 = wl2 + bank_guard[bank] + W(("dve", xn_done))
                            bank_guard[bank] = []
                        v_mm = pe(lambda eng, bank=bank, ws=wslot, it=it, kc=kc:
                                  eng.matmul(
                                      pb[bank][:],
                                      wdq_gu[:, ws * 512 + it * 128:
                                             ws * 512 + (it + 1) * 128],
                                      xn[:, kc * T_PASS:(kc + 1) * T_PASS],
                                      start=(kc == 0), stop=False),
                                  waits=wl2, inc=(it == nit_b - 1))
                    gu_wdq_free[wslot] = v_mm
                    ur = b * 16 + (kc // 4) * 2 + mat
                    us = b * 32 + (kc // 2) * 2 + mat
                    if kc % 4 == 3:
                        raw_cons[ur] = S.val("dve")
                    if kc % 2 == 1:
                        sbc_cons[us] = S.val("dve")
            bank_done = {}
            for mat in range(2):
                v_z, zslot = z_v[mat]
                for it in range(nit_b):
                    bank = mat * 4 + it
                    v_mm = pe(lambda eng, bank=bank, zslot=zslot, it=it: eng.matmul(
                        pb[bank][:],
                        zring_gu[0:32, zslot * 512 + it * 128:
                                 zslot * 512 + (it + 1) * 128],
                        xs_bf[:], start=False, stop=True),
                        waits=W(("dve", v_xs), v_z), inc=True)
                    bank_done[bank] = v_mm
                gu_z_free[zslot] = v_mm

            for it in range(nit_b):
                itg = it_g + it
                gbank, ubank = it, 4 + it
                ss = itg % 2
                v_sil = act(lambda eng, gb=gbank, ss=ss: eng.activation(
                    sil[:, ss * 512:(ss + 1) * 512], pb[gb][:], AF.Silu),
                    waits=W(("pe", bank_done[gbank]), ("dve", sil_free[ss])))
                v_silf = afence()
                if DEBUG and p == 0 and b == 21 and it == 1:
                    v_t0 = dve(lambda eng, zs=z_v[0][1]: eng.tensor_copy(
                        dbg_w[0:32, 768:896],
                        zring_gu[0:32, zs * 512 + 128: zs * 512 + 256]),
                        waits=W(("act", v_silf)))
                    v_t1 = dve(lambda eng: eng.tensor_copy(
                        dbg_g[:, 512:1024], pb[1][:]),
                        waits=W(("act", v_silf), ("dve", v_t0)))
                    dma("pl", dbg["d_wdq"], dbg_w[:], waits=W(("dve", v_t1)))
                    dma("pl", dbg["d_g0"], dbg_g[:], waits=W(("dve", v_t1)))
                v_h = dve(lambda eng, ub=ubank, ss=ss, itg=itg: eng.tensor_tensor(
                    h[:, itg * 512:(itg + 1) * 512],
                    sil[:, ss * 512:(ss + 1) * 512], pb[ub][:], ALU.mult),
                    waits=W(("act", v_silf), ("pe", bank_done[ubank])))
                v_h = dfence()
                sil_free[ss] = v_h
                bank_guard[gbank] = [("act", v_sil)]
                bank_guard[ubank] = [("dve", v_h)]
                h_done_it[itg] = v_h
            if DEBUG and p == 0 and b == 21:
                bank_guard[1].append(("dve", S.val("dve")))
            it_g += nit_b
            pump_d_chunks(((b + 1) * nd_chunks + NB - 1) // NB)
        h_done = S.val("dve")
        if DEBUG and p == 0:
            dma("pl", dbg["d_h2"], h[:], waits=W(("dve", h_done)))

        pump_d_chunks(nd_chunks)
        attn_done = S.val("dve")
        attn_pool_done = S.val("pool")

        for g_ in range(NIT):
            mslot = g_ % 4
            v_m = dma("ad", mring[:, mslot * NIT:(mslot + 1) * NIT],
                      m_in[g_ * 128:(g_ + 1) * 128, :],
                      waits=W(("pe", m_free[mslot])))
            gw = bank_guard[2] if g_ == 0 else []
            v_mm = pe(lambda eng, g_=g_, ms=mslot: eng.matmul(
                pb[2][0:NIT, :], mring[:, ms * NIT:(ms + 1) * NIT],
                h[:, g_ * 512:(g_ + 1) * 512],
                start=(g_ == 0), stop=(g_ == NIT - 1)),
                waits=W(v_m, ("dve", h_done_it[g_])) + gw,
                inc=True)
            m_free[mslot] = v_mm
        v_hs = dve(lambda eng: eng.tensor_copy(hs_bf[:], pb[2][0:NIT, :]),
                   waits=W(("pe", v_mm)))
        v_hs = dfence()
        bank_guard[2] = [("dve", v_hs)]

        nicb = (NIT + 3) // 4
        rawd_cons = {}
        sbcd_cons = {}
        rawd_v = {}
        sbcd_v = {}
        nd_uses = N_OS * nicb

        def emit_rawd(u):
            os2, icb = divmod(u, nicb)
            nic = min(4, NIT - icb * 4)
            rslot = u % 2
            prev = rawd_cons.get(u - 2)
            assert prev is not None or u < 2
            wl = W(("dve", prev if prev is not None else h_done))
            if u < 2 and p == 0:
                wl = wl + W(v_wdp, v_sdx)
            src_ap = wd_perm.rearrange("(a p) n -> p a n", p=128)[
                :, icb * 4: icb * 4 + nic, os2 * 64:(os2 + 1) * 64]
            dst = raw_d[:, rslot * 256: rslot * 256 + nic * 64].rearrange(
                "p (c n) -> p c n", c=nic)
            rawd_v[u] = [dma("ad", dst, src_ap, waits=wl)]

        def emit_sbcd(u):
            os2, icb = divmod(u, nicb)
            nic = min(4, NIT - icb * 4)
            sslot = u % 2
            prev = sbcd_cons.get(u - 2)
            assert prev is not None or u < 2
            wl = W(("dve", prev if prev is not None else h_done))
            if u < 2 and p == 0:
                wl = wl + W(v_sdx)
            src_ap = sd_exp.rearrange("(a p) n -> p a n", p=128)[
                :, icb * 4: icb * 4 + nic, os2 * 512:(os2 + 1) * 512]
            dst = sbc_d[:, sslot * 2048: sslot * 2048 + nic * 512].rearrange(
                "p (c n) -> p c n", c=nic)
            sbcd_v[u] = [dma("sp", dst, src_ap, waits=wl)]

        next_rd = [0]

        def pump_d(u_t):
            while next_rd[0] <= min(u_t, nd_uses - 1):
                emit_rawd(next_rd[0])
                emit_sbcd(next_rd[0])
                next_rd[0] += 1

        for os_ in range(N_OS):
            dnb = [(os_ % 2) * 4 + k for k in range(4)]
            zslot = os_ % 2
            v_zr = dma("sp", zring_d[0:NIT, zslot * 512:(zslot + 1) * 512],
                       zs_d_d[:, os_ * 512:(os_ + 1) * 512],
                       waits=W(("pe", dn_z_free[zslot]),
                               ("dve", h_done if os_ == 0 else 0)), sem="zd")
            for ic in range(NIT):
                icb, idx = ic // 4, ic % 4
                u = os_ * nicb + icb
                pump_d(u + 1)
                rslot = u % 2
                sslot = u % 2
                wslot = ic % 3
                raw16 = raw_d[:, rslot * 256 + idx * 64:
                              rslot * 256 + (idx + 1) * 64].bitcast(U16)
                wl = (W(*rawd_v[u]) + W(*sbcd_v[u]) +
                      W(("pe", dn_wdq_free[wslot])))
                for j in range(4):
                    dve(lambda eng, j=j, r=raw16: eng.tensor_scalar(
                        plane_dn[:, j * 128:(j + 1) * 128], r,
                        4 * j, 15, ALU.logical_shift_right, ALU.bitwise_and),
                        waits=wl)
                    wl = ()
                dfence(3)
                v_mul = dve(lambda eng, ws=wslot, ss=sslot, idx=idx:
                            eng.tensor_tensor(
                                wdq_dn[:, ws * 512:(ws + 1) * 512], plane_dn[:],
                                sbc_d[:, ss * 2048 + idx * 512:
                                      ss * 2048 + (idx + 1) * 512],
                                ALU.mult))
                v_mul = dfence()
                for tt in range(N_TT):
                    bank = dnb[tt]
                    wl2 = W(("dve", v_mul))
                    if ic == 0:
                        wl2 = wl2 + bank_guard[bank] + W(("dve", h_done))
                        bank_guard[bank] = []
                    v_mm = pe(lambda eng, bank=bank, ws=wslot, ic=ic, tt=tt:
                              eng.matmul(
                                  pb[bank][:],
                                  h[:, ic * 512 + tt * 128: ic * 512 + (tt + 1) * 128],
                                  wdq_dn[:, ws * 512:(ws + 1) * 512],
                                  start=(ic == 0), stop=False),
                              waits=wl2, inc=(tt == N_TT - 1))
                dn_wdq_free[wslot] = v_mm
                if idx == 3 or ic == NIT - 1:
                    rawd_cons[u] = S.val("dve")
                    sbcd_cons[u] = S.val("dve")
            for tt in range(N_TT):
                bank = dnb[tt]
                v_mm = pe(lambda eng, bank=bank, tt=tt, zs=zslot: eng.matmul(
                    pb[bank][:],
                    hs_bf[:, tt * 128:(tt + 1) * 128],
                    zring_d[0:NIT, zs * 512:(zs + 1) * 512],
                    start=False, stop=True),
                    waits=W(("dve", v_hs), v_zr), inc=True)
                if tt == N_TT - 1:
                    dn_z_free[zslot] = v_mm
                r0 = t0 + tt * 128
                u2 = (os_ * N_TT + tt) % 2
                v_xl = dma("ad", xsl[:, u2 * 512:(u2 + 1) * 512],
                           x_in[r0:r0 + 128, os_ * 512:(os_ + 1) * 512],
                           waits=W(("dve", xsl_free[u2])))
                v_sil = act(lambda eng, bank=bank, u2=u2: eng.activation(
                    sil[:, u2 * 512:(u2 + 1) * 512], pb[bank][:], AF.Silu),
                    waits=W(("pe", v_mm), ("dve", sil_free[u2])))
                bank_guard[bank] = [("act", v_sil)]
                v_sil = afence()
                v_sx = dve(lambda eng, tt=tt, os_=os_, u2=u2: eng.tensor_copy(
                    sexp[:, u2 * 128:(u2 + 1) * 128].rearrange(
                        "p (d m) -> p d m", m=8),
                    o_attn[:, tt * 128 + os_ * 16: tt * 128 + (os_ + 1) * 16
                           ].unsqueeze(2).broadcast_to([128, 16, 8])),
                    waits=W(("dve", sexp_free[u2]), ("dve", attn_done)))
                v_sx = dfence()
                v_sg = dve(lambda eng, u2=u2: eng.tensor_tensor(
                    sgb[:, u2 * 512:(u2 + 1) * 512].rearrange(
                        "p (j dm) -> p j dm", j=4),
                    sil[:, u2 * 512:(u2 + 1) * 512].rearrange(
                        "p (j dm) -> p j dm", j=4),
                    sexp[:, u2 * 128:(u2 + 1) * 128].unsqueeze(1).broadcast_to(
                        [128, 4, 128]),
                    ALU.mult),
                    waits=W(("act", v_sil), ("dve", v_sx), ("dve", sgb_free[u2])))
                sil_free[u2] = v_sg
                sexp_free[u2] = v_sg
                v_out = dve(lambda eng, u2=u2: eng.tensor_tensor(
                    osl[:, u2 * 512:(u2 + 1) * 512].rearrange(
                        "p (m j) -> p m j", j=4),
                    sgb[:, u2 * 512:(u2 + 1) * 512].rearrange(
                        "p (j m) -> p m j", j=4),
                    xsl[:, u2 * 512:(u2 + 1) * 512].rearrange(
                        "p (m j) -> p m j", j=4),
                    ALU.add),
                    waits=W(v_xl, ("dve", v_sg), osl_free[u2]))
                v_out = dfence()
                xsl_free[u2] = v_out
                sgb_free[u2] = v_out
                v_st = dma("pl", out_t[r0:r0 + 128, os_ * 512:(os_ + 1) * 512],
                           osl[:, u2 * 512:(u2 + 1) * 512],
                           waits=W(("dve", v_out)))
                osl_free[u2] = v_st
        if DEBUG and p == 0:
            dma("pl", dbg["d_xn"], xn[:], waits=W(("dve", S.val("dve"))))
            dma("pl", dbg["d_qkvf"], qkvf[:], waits=W(("dve", S.val("dve"))))
            dma("pl", dbg["d_oattn"], o_attn[:], waits=W(("dve", attn_done)))
            dma("pl", dbg["d_stat"], stat[:], waits=W(("dve", S.val("dve"))))
            dma("pl", dbg["d_xs"], xs_bf[:], waits=W(("dve", S.val("dve"))))
            dma("pl", dbg["d_hs"], hs_bf[:], waits=W(("dve", S.val("dve"))))
            dma("pl", dbg["d_h"], h[:], waits=W(("dve", S.val("dve"))))
        prev_pass_end = S.snap()

    S.emit("sp", lambda eng: eng.nop(),
           waits=snap_waits(S.snap(), ["pl", "ad", "sp", "sd", "sd2"]))

    with nc.Block() as block:
        def make_runner(name):
            dma_sems = set(DMA_SEM_NAMES)

            def run(eng):
                last = {}
                for (fn, waits, inc, inc_n) in S.ev[name]:
                    for (sn, val) in waits:
                        if last.get(sn, 0) >= val:
                            continue
                        eng.wait_ge(sems[sn], val * DMAW if sn in dma_sems else val)
                        last[sn] = val
                    ins = fn(eng)
                    if inc is not None:
                        ins.then_inc(sems[inc],
                                     DMAW if inc in dma_sems else inc_n)
            return run

        block.sync(make_runner("sp"))
        block.tensor(make_runner("pe"))
        block.vector(make_runner("dve"))
        block.scalar(make_runner("act"))
        block.gpsimd(make_runner("pool"))

    ctx.close()
    return nc


_CACHE = {}


def _consts():
    ident = np.eye(128, dtype=ml_dtypes.bfloat16)
    oneh = np.zeros((128, 32 * 32), dtype=ml_dtypes.bfloat16)
    for kc in range(32):
        oneh[:, kc * 32 + kc] = 1.0
    m = np.zeros((I_DIM, NIT), dtype=ml_dtypes.bfloat16)
    for r in range(I_DIM):
        m[r, _cp_nat(r) // 128] = 1.0
    return ident, oneh, m


def kernel(L0_attn_x, L0_attn_Wq, L0_attn_bq, L0_attn_Wk, L0_attn_bk,
           L0_attn_Wv, L0_attn_bv,
           L0_ffn_gate_qweight, L0_ffn_gate_scales, L0_ffn_gate_qzeros,
           L0_ffn_up_qweight, L0_ffn_up_scales, L0_ffn_up_qzeros,
           L0_ffn_down_qweight, L0_ffn_down_scales, L0_ffn_down_qzeros):
    x = np.ascontiguousarray(np.asarray(L0_attn_x, dtype=np.float32)).reshape(8192, D)
    wq = np.ascontiguousarray(np.asarray(L0_attn_Wq, dtype=np.float32)).reshape(D, D)
    wk = np.ascontiguousarray(np.asarray(L0_attn_Wk, dtype=np.float32)).reshape(D, 128)
    wv = np.ascontiguousarray(np.asarray(L0_attn_Wv, dtype=np.float32)).reshape(D, 128)
    bq = np.ascontiguousarray(np.asarray(L0_attn_bq, dtype=np.float32)).reshape(128, 32)
    bk = np.ascontiguousarray(np.asarray(L0_attn_bk, dtype=np.float32)).reshape(1, 128)
    bv = np.ascontiguousarray(np.asarray(L0_attn_bv, dtype=np.float32)).reshape(1, 128)
    gqw = np.ascontiguousarray(np.asarray(L0_ffn_gate_qweight, dtype=np.int32))
    gsc = np.ascontiguousarray(np.asarray(L0_ffn_gate_scales, dtype=np.float32))
    gqz = np.ascontiguousarray(np.asarray(L0_ffn_gate_qzeros, dtype=np.int32))
    uqw = np.ascontiguousarray(np.asarray(L0_ffn_up_qweight, dtype=np.int32))
    usc = np.ascontiguousarray(np.asarray(L0_ffn_up_scales, dtype=np.float32))
    uqz = np.ascontiguousarray(np.asarray(L0_ffn_up_qzeros, dtype=np.int32))
    dqw = np.ascontiguousarray(np.asarray(L0_ffn_down_qweight, dtype=np.int32))
    dsc = np.ascontiguousarray(np.asarray(L0_ffn_down_scales, dtype=np.float32))
    dqz = np.ascontiguousarray(np.asarray(L0_ffn_down_qzeros, dtype=np.int32))

    ident, oneh, m = _consts()
    if "nc" not in _CACHE:
        _CACHE["nc"] = build_nc()
    nc = _CACHE["nc"]
    in_maps = []
    for c in range(N_CORES):
        in_maps.append(dict(
            x=x[c * T_CORE:(c + 1) * T_CORE], wq=wq, wk=wk, wv=wv,
            bq=bq, bk=bk, bv=bv,
            g_qw=gqw, g_sc=gsc, g_qz=gqz,
            u_qw=uqw, u_sc=usc, u_qz=uqz,
            d_qw=dqw, d_sc=dsc, d_qz=dqz,
            ident=ident, onehot=oneh, mmat=m,
        ))
    res = run_bass_kernel_spmd(nc, in_maps, core_ids=list(range(N_CORES)))
    out = np.concatenate([res.results[c]["out"] for c in range(N_CORES)], axis=0)
    return out.reshape(4, 2048, D)


# version 19
# speedup vs baseline: 1.4947x; 1.0352x over previous
import contextlib
import numpy as np
import ml_dtypes
import concourse.bass as bass
import concourse.mybir as mybir
from concourse.bass_utils import run_bass_kernel_spmd

F32 = mybir.dt.float32
BF16 = mybir.dt.bfloat16
I32 = mybir.dt.int32
U16 = mybir.dt.uint16
AF = mybir.ActivationFunctionType
ALU = mybir.AluOpType
AX = mybir.AxisListType

N_CORES = 8
T_CORE = 1024
T_PASS = 512
N_PASS = 2
N_TT = 4
D = 4096
KC = 32
I_DIM = 11008
NIT = 86
O_DIM = 4096
N_OS = 8
PLANE = 2752
GAMMA = 1.0 / np.sqrt(128.0)
EPS = 1e-5
DMAW = 16

ENGINES = ("sp", "pe", "dve", "act", "pool")
DEBUG = False


def _gu_blocks():
    blocks = []
    c = 0
    while c < I_DIM:
        bw = min(512, I_DIM - c)
        segs = []
        cc = c
        while cc < c + bw:
            j = cc // PLANE
            w0 = cc - j * PLANE
            nw = min(PLANE - w0, c + bw - cc)
            segs.append((j, w0, nw))
            cc += nw
        blocks.append((c, bw, segs))
        c += bw
    return blocks


GU_BLOCKS = _gu_blocks()
NB = len(GU_BLOCKS)


def _it_segs(it):
    segs = []
    c = 128 * it
    end = c + 128
    while c < end:
        j = c // PLANE
        w0 = c - j * PLANE
        nw = min(PLANE - w0, end - c)
        segs.append((j, w0, nw))
        c += nw
    return segs


IT_SEGS = [_it_segs(it) for it in range(NIT)]


def _cp_nat(cp):
    j = cp // PLANE
    return 4 * (cp - j * PLANE) + j


DMA_LANES = {"sp": 6, "ad": 6, "pl": 6, "sd": 3, "sd2": 3,
             "zg": 2, "zd": 2, "rp": 2, "su": 2}
DMA_SEM_NAMES = [b + str(i) for b, n in DMA_LANES.items() for i in range(n)]
COMPUTE_SEMS = ["pe", "dve", "act", "pool"]


class Sched:
    def __init__(self):
        self.ev = {e: [] for e in ENGINES}
        self.cnt = {n: 0 for n in DMA_SEM_NAMES + COMPUTE_SEMS}
        self.rr = {b: 0 for b in DMA_LANES}

    def emit(self, eng, fn, waits=(), inc=None, inc_n=1):
        if inc is not None:
            self.cnt[inc] += inc_n
        self.ev[eng].append((fn, tuple(waits), inc, inc_n))
        return self.cnt[inc] if inc is not None else None

    def val(self, sem):
        return self.cnt[sem]

    def snap(self):
        return dict(self.cnt)


def W(*pairs):
    out = []
    for p in pairs:
        if p is None:
            continue
        n, v = p
        if v:
            out.append((n, v))
    return out


def snap_waits(snap, bases):
    return [(b + str(i), snap[b + str(i)])
            for b in bases for i in range(DMA_LANES[b])
            if snap[b + str(i)]]


def build_nc():
    nc = bass.Bass("TRN2", target_bir_lowering=False, debug=False)

    def din(name, shape, dt):
        return nc.dram_tensor(name, shape, dt, kind="ExternalInput").ap()

    x_in = din("x", [T_CORE, D], F32)
    wq_in = din("wq", [D, D], F32)
    wk_in = din("wk", [D, 128], F32)
    wv_in = din("wv", [D, 128], F32)
    bq_in = din("bq", [128, 32], F32)
    bk_in = din("bk", [1, 128], F32)
    bv_in = din("bv", [1, 128], F32)
    gq_in = din("g_qw", [D, 1376], I32)
    gs_in = din("g_sc", [32, I_DIM], F32)
    gz_in = din("g_qz", [32, 1376], I32)
    uq_in = din("u_qw", [D, 1376], I32)
    us_in = din("u_sc", [32, I_DIM], F32)
    uz_in = din("u_qz", [32, 1376], I32)
    dq_in = din("d_qw", [I_DIM, 512], I32)
    ds_in = din("d_sc", [NIT, O_DIM], F32)
    dz_in = din("d_qz", [NIT, 512], I32)
    ident_in = din("ident", [128, 128], BF16)
    oneh_in = din("onehot", [128, 32 * 32], BF16)
    m_in = din("mmat", [I_DIM, NIT], BF16)

    out_t = nc.dram_tensor("out", [T_CORE, D], F32, kind="ExternalOutput").ap()
    if DEBUG:
        dbg = {
            "d_xn": nc.dram_tensor("d_xn", [128, KC * T_PASS], F32, kind="ExternalOutput").ap(),
            "d_wqkv": nc.dram_tensor("d_wqkv", [128, KC * 384], F32, kind="ExternalOutput").ap(),
            "d_qkvf": nc.dram_tensor("d_qkvf", [128, N_TT * 384], F32, kind="ExternalOutput").ap(),
            "d_oattn": nc.dram_tensor("d_oattn", [128, N_TT * 128], F32, kind="ExternalOutput").ap(),
            "d_stat": nc.dram_tensor("d_stat", [128, 64], F32, kind="ExternalOutput").ap(),
            "d_xs": nc.dram_tensor("d_xs", [32, T_PASS], F32, kind="ExternalOutput").ap(),
            "d_hs": nc.dram_tensor("d_hs", [NIT, T_PASS], F32, kind="ExternalOutput").ap(),
            "d_h": nc.dram_tensor("d_h", [128, NIT * T_PASS], F32, kind="ExternalOutput").ap(),
            "d_wdq": nc.dram_tensor("d_wdqg", [128, 2 * 512], BF16, kind="ExternalOutput").ap(),
            "d_h2": nc.dram_tensor("d_h2", [128, NIT * T_PASS], BF16, kind="ExternalOutput").ap(),
            "d_g0": nc.dram_tensor("d_g0", [128, 2 * 512], BF16, kind="ExternalOutput").ap(),
        }

    def dscr(name, shape, dt):
        return nc.dram_tensor(name, shape, dt, kind="Internal").ap()

    sprep_g_d = dscr("sprep_g", [32, I_DIM], BF16)
    sprep_u_d = dscr("sprep_u", [32, I_DIM], BF16)
    sprep_d_d = dscr("sprep_d", [NIT, O_DIM], BF16)
    zs_g_d = dscr("zsneg_g", [32, I_DIM], BF16)
    zs_u_d = dscr("zsneg_u", [32, I_DIM], BF16)
    zs_d_d = dscr("zsneg_d", [NIT, O_DIM], BF16)
    wqkv_d = dscr("wqkv_scr", [128, KC * 384], BF16)
    sd_exp = dscr("sd_exp", [NIT * 128, O_DIM], BF16)
    wd_perm = dscr("wd_perm", [I_DIM, 512], I32)
    bqs_d = dscr("bqs_scr", [128, 1], F32)

    ctx = contextlib.ExitStack()

    def sb(name, shape, dt):
        return ctx.enter_context(nc.sbuf_tensor(name, shape, dt))

    xn = sb("xn", [128, KC * T_PASS], BF16)
    h = sb("h", [128, NIT * T_PASS], BF16)
    ident_s = sb("ident_s", [128, 128], BF16)
    oneh_s = sb("oneh_s", [128, 32 * 32], BF16)
    bias3 = sb("bias3", [128, 384], F32)
    xs_bf = sb("xs_bf", [32, T_PASS], BF16)
    hs_bf = sb("hs_bf", [NIT, T_PASS], BF16)
    o_attn = sb("o_attn", [128, N_TT * 128], F32)
    qkvf = sb("qkvf", [128, N_TT * 384], F32)
    stat = sb("stat", [128, 64], F32)
    mring = sb("mring", [128, 4 * NIT], BF16)
    kab = sb("kab", [128, 4], F32)
    sexp = sb("sexp", [128, 2 * 128], F32)
    zring_gu = sb("zring_gu", [32, 4 * 512], BF16)
    zring_d = sb("zring_d", [NIT, 2 * 512], BF16)
    wdq_gu = sb("wdq_gu", [128, 4 * 512], BF16)
    wdq_dn = sb("wdq_dn", [128, 3 * 512], BF16)
    plane_gu = sb("plane_gu", [128, 2 * 512], U16)
    plane_dn = sb("plane_dn", [128, 512], U16)

    UA = sb("UA", [128, 8192], I32)
    UAf = UA[:].bitcast(F32)
    UAb = UA[:].bitcast(BF16)
    UAu = UA[:].bitcast(U16)
    xa = UAf[:, 0:2048]
    xb = UAf[:, 2048:4096]
    junk = UAf[:, 4096:6144]
    xn_tm = UAb[:, 12288:16384]
    raw_gu = UA[:][:, 0:4096]
    sbc_gu = UAb[:, 8192:12288]
    raw_d = UA[:][:, 0:512]
    sbc_d = UAb[:, 1024:5120]
    xsl = UAf[:, 2560:3584]
    osl = UAf[:, 3584:4608]
    sgb = UAf[:, 4608:5632]
    sil = UAf[:, 6144:7168]
    dbg_w = UAb[:, 14336:15360]
    dbg_g = UAb[:, 15360:16384]
    st_sc = UAf[:, 0:1376]
    st_sc2 = UAf[:, 1376:2752]
    st_qz = UA[:][:, 768:2144]
    st_zp = UAu[:, 4288:7040]
    st_zc = UAb[:, 7040:9792]

    UB = sb("UB", [128, 6144], I32)
    UBb = UB[:].bitcast(BF16)
    UBf = UB[:].bitcast(F32)
    sprep_sb = UBb[:, 0:I_DIM]
    wqkv = UBb[:, 0:KC * 384]
    sc_t = UBf[:, 0:1024]
    et_t = UBf[:, 1024:2048]
    zn_t = UBf[:, 2048:2816]

    pb = [ctx.enter_context(nc.psum_tensor(f"pb{i}", [128, 512], F32)) for i in range(8)]

    S = Sched()
    sems = {n: ctx.enter_context(nc.semaphore("sem_" + n))
            for n in DMA_SEM_NAMES + COMPUTE_SEMS}

    def dma(queue, dst, src, waits=(), sem=None):
        eng_name = {"sp": "sp", "ad": "act", "pl": "pool"}[queue]
        base = sem or queue
        lane = base + str(S.rr[base] % DMA_LANES[base])
        S.rr[base] += 1
        prev = S.cnt[lane]
        wl = list(waits)
        if prev:
            wl.append((lane, prev))

        def fn(eng, dst=dst, src=src):
            return eng.dma_start(out=dst, in_=src)
        return (lane, S.emit(eng_name, fn, waits=wl, inc=lane, inc_n=1))

    def pe(fn, waits=(), inc=False):
        return S.emit("pe", fn, waits=waits, inc="pe" if inc else None)

    def dve(fn, waits=(), inc=True):
        return S.emit("dve", fn, waits=waits, inc="dve" if inc else None)

    def act(fn, waits=(), inc=True):
        return S.emit("act", fn, waits=waits, inc="act" if inc else None)

    def pool(fn, waits=(), inc=True):
        return S.emit("pool", fn, waits=waits, inc="pool" if inc else None)

    def dfence(n=2):
        for _ in range(n - 1):
            S.emit("dve", lambda eng: eng.engine_nop(), inc="dve")
        return S.emit("dve", lambda eng: eng.engine_nop(), inc="dve")

    def afence(n=2):
        for _ in range(n - 1):
            S.emit("act", lambda eng: eng.mul(kab[0:1, 2:3], kab[0:1, 2:3], 1.0),
                   inc="act")
        return S.emit("act", lambda eng: eng.mul(kab[0:1, 2:3], kab[0:1, 2:3], 1.0),
                      inc="act")

    def plfence(n=2):
        for _ in range(n - 1):
            S.emit("pool", lambda eng: eng.engine_nop(), inc="pool")
        return S.emit("pool", lambda eng: eng.engine_nop(), inc="pool")

    def tiny(fn, waits=()):
        S.emit("dve", fn, waits=waits, inc="dve")
        S.emit("dve", lambda eng: eng.engine_nop(), inc="dve")
        return S.emit("dve", lambda eng: eng.engine_nop(), inc="dve")

    for gi_, (sc_in, qz_inp, sp_dram, zs_dram) in enumerate(
            [(gs_in, gz_in, sprep_g_d, zs_g_d), (us_in, uz_in, sprep_u_d, zs_u_d)]):
        v_sp = None
        ad_bar = S.snap()
        dve_bar = S.val("dve")
        for c4 in range(4):
            v_l = dma("sp", st_sc[0:32, :],
                      sc_in[:, c4 * 2752: c4 * 2752 + 1376],
                      waits=W(("dve", v_sp if v_sp else dve_bar)) +
                            (snap_waits(ad_bar, ["ad"]) if c4 == 0 else []), sem="su")
            v_l2 = dma("sp", st_sc2[0:32, :],
                       sc_in[:, c4 * 2752 + 1376:(c4 + 1) * 2752],
                       waits=W(("dve", v_sp if v_sp else dve_bar)), sem="su")
            for j in range(4):
                src1 = st_sc[0:32, :].rearrange("p (w f) -> p w f", f=4)[:, :, j]
                src2 = st_sc2[0:32, :].rearrange("p (w f) -> p w f", f=4)[:, :, j]
                d0 = j * PLANE + c4 * 688
                dve(lambda eng, s=src1, d0=d0: eng.tensor_copy(
                    sprep_sb[0:32, d0:d0 + 344], s),
                    waits=W(v_l))
                v_sp = dve(lambda eng, s=src2, d0=d0: eng.tensor_copy(
                    sprep_sb[0:32, d0 + 344:d0 + 688], s),
                    waits=W(v_l2))
        v_spw = dma("ad", sp_dram, sprep_sb[0:32, :], waits=W(("dve", v_sp)))
        v_qz = dma("sp", st_qz[0:32, :], qz_inp, waits=W(("dve", v_sp)), sem="su")
        qz16 = st_qz[0:32, :].bitcast(U16)
        v_zsw = None
        for j in range(4):
            dve(lambda eng, q=qz16, j=j: eng.tensor_scalar(
                st_zp[0:32, :], q, 4 * j, 15,
                ALU.logical_shift_right, ALU.bitwise_and),
                waits=W(v_qz, v_zsw))
            v_zs = dve(lambda eng, j=j: eng.scalar_tensor_tensor(
                st_zc[0:32, :], st_zp[0:32, :], -1.0,
                sprep_sb[0:32, j * PLANE:(j + 1) * PLANE], ALU.mult, ALU.mult))
            v_zsw = dma("ad", zs_dram[:, j * PLANE:(j + 1) * PLANE],
                        st_zc[0:32, :], waits=W(("dve", v_zs)))

    v_dsp = None
    chunk_v = [None, None, None]
    for ck in range(3):
        n0 = ck * 1376
        nn = min(1376, O_DIM - n0)
        v_l = dma("sp", st_sc[0:NIT, 0:nn], ds_in[:, n0:n0 + nn],
                  waits=W(("dve", v_dsp), v_zsw), sem="su")
        for s_ in range(8):
            for j in range(4):
                mlo = max(0, -(-(n0 - 512 * s_ - j) // 4))
                mhi = min(128, (n0 + nn - 512 * s_ - j + 3) // 4)
                if mlo >= mhi:
                    continue
                w0 = 512 * s_ + 4 * mlo + j - n0
                srcv = st_sc[0:NIT, w0:w0 + 4 * (mhi - mlo - 1) + 1:4]
                d0 = 512 * s_ + 128 * j + mlo
                v_dsp = dve(lambda eng, s=srcv, d0=d0, n=mhi - mlo:
                            eng.tensor_copy(sprep_sb[0:NIT, d0:d0 + n], s),
                            waits=W(v_l))
    v_dsp = dfence()
    v_spdw = dma("ad", sprep_d_d, sprep_sb[0:NIT, 0:O_DIM], waits=W(("dve", v_dsp)))
    v_dqz = dma("sp", st_qz[0:NIT, 0:512], dz_in, waits=W(("dve", v_dsp)), sem="su")
    qzd16 = st_qz[0:NIT, 0:512].bitcast(U16)
    v_zdw = None
    for s_ in range(8):
        for j in range(4):
            dve(lambda eng, s_=s_, j=j: eng.tensor_scalar(
                st_zp[0:NIT, 0:128], qzd16[:, 128 * s_:128 * s_ + 128],
                4 * j, 15, ALU.logical_shift_right, ALU.bitwise_and),
                waits=W(v_dqz, v_zdw))
            dfence(3)
            v_zd = dve(lambda eng, s_=s_, j=j: eng.scalar_tensor_tensor(
                st_zc[0:NIT, 0:128], st_zp[0:NIT, 0:128], -1.0,
                sprep_sb[0:NIT, s_ * 512 + j * 128: s_ * 512 + (j + 1) * 128],
                ALU.mult, ALU.mult))
            v_zd = dfence()
            v_zdw = dma("ad", zs_d_d[:, s_ * 512 + j * 128: s_ * 512 + (j + 1) * 128],
                        st_zc[0:NIT, 0:128], waits=W(("dve", v_zd)))

    v_sdx = None
    for ic in range(NIT):
        poff = 0
        for (j, w0, nw) in IT_SEGS[ic]:
            pos = 0
            while pos < nw:
                gg = (4 * (w0 + pos) + j) // 128
                run = min(nw - pos, 32 - ((w0 + pos) % 32))
                srcap = sprep_d_d[gg:gg + 1, :].broadcast_to([run, O_DIM])
                dst = sd_exp[ic * 128 + poff + pos: ic * 128 + poff + pos + run, :]
                v_sdx = dma("sp", dst, srcap, waits=W(v_spdw), sem="su")
                pos += run
            poff += nw
    v_wdp = None
    for j in range(4):
        srcap = dq_in.rearrange("(w f) n -> w f n", f=4)[:, j, :]
        dst = wd_perm[j * PLANE:(j + 1) * PLANE, :]
        v_wdp = dma("sp", dst, srcap, sem="su")

    v_id = dma("sp", ident_s[:], ident_in)
    v_oh = dma("sp", oneh_s[:], oneh_in)

    v_red = None
    for kc in range(KC):
        wl = W(v_zdw) if kc == 0 else W(("dve", v_red))
        v_l = dma("ad", xa, wq_in[kc * 128:(kc + 1) * 128, 0:2048], waits=wl)
        v_l2 = dma("ad", xb, wq_in[kc * 128:(kc + 1) * 128, 2048:4096])
        v_l3 = dma("ad", junk[:, 0:128], wk_in[kc * 128:(kc + 1) * 128, :])
        v_l4 = dma("ad", junk[:, 128:256], wv_in[kc * 128:(kc + 1) * 128, :])
        tiny(lambda eng: eng.reduce_sum(
            junk[:, 512:576].unsqueeze(2),
            xa.rearrange("p (d f) -> p d f", f=32), AX.X),
            waits=W(v_l))
        v_r2 = tiny(lambda eng: eng.reduce_sum(
            junk[:, 576:640].unsqueeze(2),
            xb.rearrange("p (d f) -> p d f", f=32), AX.X),
            waits=W(v_l2))
        dve(lambda eng, kc=kc: eng.tensor_scalar(
            wqkv[:, kc * 384: kc * 384 + 128], junk[:, 512:640],
            GAMMA, None, ALU.mult), waits=W(("dve", v_r2)))
        v_red = dve(lambda eng, kc=kc: eng.tensor_copy(
            wqkv[:, kc * 384 + 128: kc * 384 + 384], junk[:, 0:256]),
            waits=W(v_l4))
    v_red = dfence()
    v_wqkv_w = dma("ad", wqkv_d, wqkv[:], waits=W(("dve", v_red)))

    v_bq = dma("ad", xa[:, 0:32], bq_in, waits=W(("dve", v_red)))
    v_br = tiny(lambda eng: eng.reduce_sum(xb[:, 0:1].unsqueeze(2),
                                           xa[:, 0:32].unsqueeze(1), AX.X),
                waits=W(v_bq))
    v_bqs = tiny(lambda eng: eng.tensor_scalar(
        xb[:, 1:2], xb[:, 0:1], GAMMA, None, ALU.mult),
        waits=W(("dve", v_br)))
    v_bqsw = dma("ad", bqs_d, xb[:, 1:2], waits=W(("dve", v_bqs)))
    v_b1 = dma("sp", bias3[:, 0:128],
               bqs_d.rearrange("p one -> one p").partition_broadcast(128).squeeze(1),
               waits=W(v_bqsw))
    v_b2 = dma("sp", bias3[:, 128:256], bk_in.partition_broadcast(128).squeeze(1))
    v_b3 = dma("sp", bias3[:, 256:384], bv_in.partition_broadcast(128).squeeze(1))
    setup_end = S.snap()

    bank_guard = [[] for _ in range(8)]
    gu_raw_free = [None] * 4
    gu_sbc_free = [None] * 4
    gu_wdq_free = [None] * 4
    gu_z_free = [None] * 4
    dn_raw_free = [None] * 2
    dn_sbc_free = [None] * 2
    dn_wdq_free = [None] * 3
    dn_z_free = [None] * 2
    m_free = [None] * 4
    sil_free = [None] * 2
    xsl_free = [None] * 2
    osl_free = [None] * 2
    sgb_free = [None] * 2
    sexp_free = [None] * 2
    prev_pass_end = None

    for p in range(N_PASS):
        t0 = p * T_PASS

        xn_ready_tt = [None] * N_TT
        half_free = [setup_end["act"], setup_end["act"]]
        tm_free = [None, None]
        for tt in range(N_TT):
            r0 = t0 + tt * 128
            half_loads = []
            v_sq = [None, None]
            for hf in range(2):
                buf = (xa, xb)[hf]
                wl = W(("act", half_free[hf]))
                if tt == 0:
                    if p == 0:
                        wl = wl + snap_waits(setup_end, ["sp", "su"]) + W(("dve", setup_end["dve"]))
                    else:
                        wl = wl + W(("dve", prev_pass_end["dve"])) + \
                            snap_waits(prev_pass_end, ["pl", "sd", "sd2"])
                v_x = dma("ad", buf,
                          x_in[r0:r0 + 128, hf * 2048:(hf + 1) * 2048], waits=wl)
                half_loads.append(v_x)
                dve(lambda eng, b=buf, tt=tt, hf=hf: eng.reduce_sum(
                    stat[:, tt * 16 + hf: tt * 16 + hf + 1].unsqueeze(2),
                    b.unsqueeze(1), AX.X),
                    waits=W(v_x))
                jw = W(v_x)
                if p == 0 and tt == 0:
                    jw = jw + snap_waits(setup_end, ["sp", "su"])
                v_sq[hf] = act(lambda eng, b=buf, tt=tt, hf=hf: eng.activation(
                    junk, b, AF.Square,
                    accum_out=stat[:, tt * 16 + 2 + hf: tt * 16 + 3 + hf]),
                    waits=jw)
            c0 = tt * 16
            v_c = tiny(lambda eng, c0=c0: eng.tensor_tensor(
                stat[:, c0 + 4:c0 + 5], stat[:, c0:c0 + 1],
                stat[:, c0 + 1:c0 + 2], ALU.add),
                waits=W(("act", v_sq[1])))
            v_c = tiny(lambda eng, c0=c0: eng.tensor_tensor(
                stat[:, c0 + 5:c0 + 6], stat[:, c0 + 2:c0 + 3],
                stat[:, c0 + 3:c0 + 4], ALU.add), waits=W(("dve", v_c)))
            v_c = tiny(lambda eng, c0=c0: eng.tensor_scalar(
                stat[:, c0 + 6:c0 + 7], stat[:, c0 + 4:c0 + 5],
                1.0 / D, None, ALU.mult), waits=W(("dve", v_c)))
            v_c = tiny(lambda eng, c0=c0: eng.tensor_scalar(
                stat[:, c0 + 7:c0 + 8], stat[:, c0 + 5:c0 + 6],
                1.0 / D, None, ALU.mult), waits=W(("dve", v_c)))
            v_c = tiny(lambda eng, c0=c0: eng.tensor_tensor(
                stat[:, c0 + 8:c0 + 9], stat[:, c0 + 6:c0 + 7],
                stat[:, c0 + 6:c0 + 7], ALU.mult), waits=W(("dve", v_c)))
            v_c = tiny(lambda eng, c0=c0: eng.tensor_tensor(
                stat[:, c0 + 9:c0 + 10], stat[:, c0 + 7:c0 + 8],
                stat[:, c0 + 8:c0 + 9], ALU.subtract), waits=W(("dve", v_c)))
            v_var = tiny(lambda eng, c0=c0: eng.tensor_scalar(
                stat[:, c0 + 10:c0 + 11], stat[:, c0 + 9:c0 + 10],
                EPS, None, ALU.add), waits=W(("dve", v_c)))
            act(lambda eng, c0=c0: eng.activation(
                stat[:, c0 + 11:c0 + 12], stat[:, c0 + 10:c0 + 11], AF.Ln),
                waits=W(("dve", v_var)))
            v_rstd = act(lambda eng, c0=c0: eng.activation(
                stat[:, c0 + 12:c0 + 13], stat[:, c0 + 11:c0 + 12],
                AF.Exp, scale=-0.5))
            v_nb = tiny(lambda eng, c0=c0: eng.scalar_tensor_tensor(
                stat[:, c0 + 13:c0 + 14], stat[:, c0 + 6:c0 + 7], -1.0,
                stat[:, c0 + 12:c0 + 13], ALU.mult, ALU.mult),
                waits=W(("act", v_rstd)))
            last_ev = None
            for hf in range(2):
                sl = hf
                wl = W(("dve", v_nb), half_loads[hf],
                       ("pe", tm_free[sl]))
                v_tm = act(lambda eng, b=(xa, xb)[hf], c0=c0, sl=sl: eng.activation(
                    xn_tm[:, sl * 2048:(sl + 1) * 2048], b, AF.Identity,
                    bias=stat[:, c0 + 13:c0 + 14], scale=stat[:, c0 + 12:c0 + 13]),
                    waits=wl)
                v_tm = afence()
                half_free[hf] = v_tm
                for k2 in range(16):
                    kc = hf * 16 + k2
                    bank = kc % 4
                    pview = pb[bank][:].bitcast(BF16)
                    g = bank_guard[bank]
                    bank_guard[bank] = []
                    v_mm = pe(lambda eng, pv=pview, k2=k2, sl=sl:
                              eng.transpose(pv[:, 0:128],
                                            xn_tm[:, sl * 2048 + k2 * 128:
                                                  sl * 2048 + (k2 + 1) * 128],
                                            ident_s[:]),
                              waits=W(("act", v_tm)) + g, inc=True)
                    v_ev = dve(lambda eng, pv=pview, kc=kc, tt=tt:
                               eng.tensor_copy(
                                   xn[:, kc * T_PASS + tt * 128:
                                      kc * T_PASS + (tt + 1) * 128],
                                   pv[:, 0:128]),
                               waits=W(("pe", v_mm)))
                    bank_guard[bank].append(("dve", v_ev))
                    last_ev = v_ev
                tm_free[sl] = S.val("pe")
            xn_ready_tt[tt] = dfence()
        xn_done = S.val("dve")

        for kc in range(KC):
            g = bank_guard[2] if kc == 0 else []
            ow = snap_waits(setup_end, ["sp"]) if (p == 0 and kc == 0) else []
            v_mm = pe(lambda eng, kc=kc: eng.matmul(
                pb[2][0:32, :], oneh_s[:, kc * 32:(kc + 1) * 32],
                xn[:, kc * T_PASS:(kc + 1) * T_PASS],
                start=(kc == 0), stop=(kc == KC - 1)),
                waits=W(("dve", xn_done)) + g + ow, inc=(kc == KC - 1))
        v_xs = dve(lambda eng: eng.tensor_copy(xs_bf[:], pb[2][0:32, :]),
                   waits=W(("pe", v_mm)))
        v_xs = dfence()
        bank_guard[2] = [("dve", v_xs)]

        if p > 0:
            v_wl = dma("ad", wqkv[:], wqkv_d,
                       waits=W(("dve", prev_pass_end["dve"]),
                               ("pool", prev_pass_end["pool"]),
                               ("act", prev_pass_end["act"])))
            wq_wait = W(v_wl)
        else:
            wq_wait = W(("dve", v_red))
        qkv_done_tt = [None] * N_TT
        for tt in range(N_TT):
            g = bank_guard[3]
            bank_guard[3] = []
            for kc in range(KC):
                extra = wq_wait + W(v_b1, v_b2, v_b3) if (tt == 0 and kc == 0) else []
                v_mm = pe(lambda eng, tt=tt, kc=kc: eng.matmul(
                    pb[3][:, 0:384],
                    xn[:, kc * T_PASS + tt * 128: kc * T_PASS + (tt + 1) * 128],
                    wqkv[:, kc * 384:(kc + 1) * 384],
                    start=(kc == 0), stop=(kc == KC - 1)),
                    waits=W(("dve", xn_ready_tt[tt])) +
                          (g if kc == 0 else []) + extra,
                    inc=(kc == KC - 1))
            v_q = dve(lambda eng, tt=tt: eng.tensor_tensor(
                qkvf[:, tt * 384:(tt + 1) * 384], pb[3][:, 0:384],
                bias3[:], ALU.add),
                waits=W(("pe", v_mm)))
            v_q = dfence()
            bank_guard[3] = [("dve", v_q)]
            qkv_done_tt[tt] = v_q
        qkv_all_pe = S.val("pe")
        if DEBUG and p == 0:
            v_dwq = dma("pl", dbg["d_wqkv"], wqkv[:], waits=W(("pe", qkv_all_pe)))
        else:
            v_dwq = None

        def d_phase():
          for tt in range(N_TT):
              q0 = tt * 384
              v_km = tiny(lambda eng, q0=q0: eng.reduce_max(
                  kab[:, 0:1].unsqueeze(2),
                  qkvf[:, q0 + 128:q0 + 256].unsqueeze(1),
                  AX.X, apply_absolute_value=True),
                  waits=W(("dve", qkv_done_tt[tt]), ("pe", qkv_all_pe)))
              v_kn = tiny(lambda eng: eng.tensor_scalar(
                  kab[:, 1:2], kab[:, 0:1], -1.0, None, ALU.mult),
                  waits=W(("dve", v_km)))
              aw = W(("dve", qkv_done_tt[tt]), ("pe", qkv_all_pe),
                     v_dwq)
              v_qa = act(lambda eng, q0=q0: eng.activation(
                  zn_t[:, 512:640], qkvf[:, q0:q0 + 128], AF.Abs), waits=aw)
              v_cn = dve(lambda eng: eng.tensor_scalar(
                  zn_t[:, 640:768], zn_t[:, 512:640], kab[:, 1:2], None, ALU.mult),
                  waits=W(("act", v_qa), ("dve", v_kn)))
              yield
              v_ev_prev = None
              v_zred_prev = None
              for db in range(16):
                  v_sc = None
                  for dd in range(8):
                      d = db * 8 + dd
                      v_sc = dve(lambda eng, q0=q0, d=d, dd=dd: eng.tensor_scalar(
                          sc_t[:, dd * 128:(dd + 1) * 128],
                          qkvf[:, q0 + 128:q0 + 256],
                          qkvf[:, q0 + d:q0 + d + 1],
                          zn_t[:, 640 + d:641 + d],
                          ALU.mult, ALU.add),
                          waits=W(("dve", v_cn), ("pool", v_ev_prev)))
                  v_sc = dfence()
                  v_e = act(lambda eng: eng.activation(et_t, sc_t, AF.Exp),
                            waits=W(("dve", v_sc)) +
                                  (W(("dve", v_zred_prev)) if v_zred_prev else []))
                  v_e = afence()
                  v_ev = pool(lambda eng, q0=q0: eng.tensor_tensor(
                      sc_t.rearrange("p (a b) -> p a b", b=128),
                      et_t.rearrange("p (a b) -> p a b", b=128),
                      qkvf[:, q0 + 256:q0 + 384].unsqueeze(1).broadcast_to([128, 8, 128]),
                      ALU.mult),
                      waits=W(("act", v_e)))
                  v_zred_prev = dve(lambda eng, db=db: eng.tensor_reduce(
                      zn_t[:, db * 8:(db + 1) * 8].unsqueeze(2),
                      et_t.rearrange("p (a b) -> p a b", b=128), AX.X, ALU.add),
                      waits=W(("act", v_e)))
                  v_ev = plfence()
                  v_nred = dve(lambda eng, db=db: eng.tensor_reduce(
                      zn_t[:, 128 + db * 8:128 + (db + 1) * 8].unsqueeze(2),
                      sc_t.rearrange("p (a b) -> p a b", b=128), AX.X, ALU.add),
                      waits=W(("pool", v_ev)))
                  v_ev_prev = v_ev
                  yield
              v_nred = dfence()
              act(lambda eng: eng.activation(zn_t[:, 256:384], zn_t[:, 0:128], AF.Ln),
                  waits=W(("dve", v_nred)))
              v_zi = act(lambda eng: eng.activation(
                  zn_t[:, 384:512], zn_t[:, 256:384], AF.Exp, scale=-1.0))
              dve(lambda eng, tt=tt: eng.tensor_tensor(
                  o_attn[:, tt * 128:(tt + 1) * 128], zn_t[:, 128:256],
                  zn_t[:, 384:512], ALU.mult),
                  waits=W(("act", v_zi)))
              yield
        dgen = d_phase()
        nd_chunks = N_TT * 18
        nd_done = [0]

        def pump_d_chunks(tgt):
            while nd_done[0] < tgt and next(dgen, "END") != "END":
                nd_done[0] += 1

        it_g = 0
        h_done_it = [None] * NIT
        a_end_act = S.val("act")
        raw_cons = {}
        sbc_cons = {}
        raw_v = {}
        sbc_v = {}
        nraw_uses = NB * 16
        nsbc_uses = NB * 32

        def emit_raw(u):
            b2, rem = divmod(u, 16)
            grp, mat = divmod(rem, 2)
            c0b, bw, segs = GU_BLOCKS[b2]
            slot = u % 4
            prev = raw_cons.get(u - 4)
            assert prev is not None or u < 4
            wl = W(("dve", prev))
            if prev is None:
                wl = wl + W(("act", a_end_act))
            q_inp = (gq_in, uq_in)[mat]
            vs = []
            off = 0
            for (j, w0, nw) in segs:
                i0, ni = w0 // 2, nw // 2
                srcap = q_inp.rearrange("(c p) n -> c p n", p=128)[
                    grp * 4:(grp + 1) * 4, :, i0:i0 + ni].transpose([1, 0, 2])
                dst = raw_gu[:, slot * 1024:(slot + 1) * 1024].rearrange(
                    "p (c n) -> p c n", c=4)[:, :, off:off + ni]
                vs.append(dma("sp", dst, srcap, waits=wl))
                wl = ()
                off += ni
            raw_v[u] = vs

        def emit_sbc(u):
            b2, rem = divmod(u, 32)
            grp, mat = divmod(rem, 2)
            c0b, bw, segs = GU_BLOCKS[b2]
            slot = u % 4
            prev = sbc_cons.get(u - 4)
            assert prev is not None or u < 4
            wl = W(("dve", prev))
            if prev is None:
                wl = wl + W(("act", a_end_act))
            sp_dram = (sprep_g_d, sprep_u_d)[mat]
            srcap = sp_dram[grp * 2:(grp + 1) * 2, c0b:c0b + bw].partition_broadcast(128)
            dst = sbc_gu[:, slot * 1024: slot * 1024 + 2 * bw].rearrange(
                "p (c n) -> p c n", c=2)
            q = ("sp", "ad")[mat]
            sbc_v[u] = dma(q, dst, srcap, waits=wl)

        next_raw = [0]
        next_sbc = [0]

        def pump(ur_t, us_t):
            while next_raw[0] <= min(ur_t, nraw_uses - 1):
                emit_raw(next_raw[0])
                next_raw[0] += 1
            while next_sbc[0] <= min(us_t, nsbc_uses - 1):
                emit_sbc(next_sbc[0])
                next_sbc[0] += 1

        for b, (c0b, bw, segs) in enumerate(GU_BLOCKS):
            nit_b = bw // 128
            z_v = {}
            for mat in range(2):
                use = b * 2 + mat
                slot = use % 4
                zs_dram = (zs_g_d, zs_u_d)[mat]
                v = dma("sp", zring_gu[0:32, slot * 512: slot * 512 + bw],
                        zs_dram[:, c0b:c0b + bw],
                        waits=W(("pe", gu_z_free[slot])), sem="zg")
                z_v[mat] = (v, slot)
            for kc in range(KC):
                minfo = {}
                for mat in range(2):
                    ur = b * 16 + (kc // 4) * 2 + mat
                    us = b * 32 + (kc // 2) * 2 + mat
                    pump(ur + 2, us + 2)
                    use = b * 64 + kc * 2 + mat
                    wslot = use % 4
                    pslot = use % 2
                    rslot = ur % 4
                    sslot = us % 4
                    raw16 = raw_gu[:, rslot * 1024:(rslot + 1) * 1024].bitcast(U16).rearrange(
                        "p (c n) -> p c n", c=4)[:, kc % 4, :]
                    wl = W(*raw_v[ur]) + W(sbc_v[us],
                           ("pe", gu_wdq_free[wslot]))
                    off = 0
                    ioff = 0
                    for (j, w0, nw) in segs:
                        dve(lambda eng, j=j, nw=nw, off=off, ioff=ioff, r=raw16, ps=pslot:
                            eng.tensor_scalar(
                                plane_gu[:, ps * 512 + off: ps * 512 + off + nw],
                                r[:, ioff:ioff + nw],
                                4 * j, 15, ALU.logical_shift_right, ALU.bitwise_and),
                            waits=wl)
                        wl = ()
                        off += nw
                        ioff += nw
                    minfo[mat] = (wslot, sslot, pslot)
                if bw < 512 or len(segs) > 1:
                    dfence(2)
                for mat in range(2):
                    wslot, sslot, pslot = minfo[mat]
                    v_mul = dve(lambda eng, ws=wslot, ss=sslot, kc=kc, bw=bw, ps=pslot:
                                eng.tensor_tensor(
                                    wdq_gu[:, ws * 512: ws * 512 + bw],
                                    plane_gu[:, ps * 512: ps * 512 + bw],
                                    sbc_gu[:, ss * 1024 + (kc % 2) * bw:
                                           ss * 1024 + (kc % 2 + 1) * bw],
                                    ALU.mult))
                    minfo[mat] = wslot
                v_mulf = dfence()
                for mat in range(2):
                    wslot = minfo[mat]
                    if DEBUG and p == 0 and b == 21 and kc == 0 and mat == 0:
                        dve(lambda eng, ws=wslot: eng.tensor_copy(
                            dbg_w[:, 0:256], wdq_gu[:, ws * 512: ws * 512 + 256]),
                            waits=W(("dve", v_mulf)))
                    for it in range(nit_b):
                        bank = mat * 4 + it
                        wl2 = W(("dve", v_mulf))
                        if kc == 0:
                            wl2 = wl2 + bank_guard[bank] + W(("dve", xn_done))
                            bank_guard[bank] = []
                        v_mm = pe(lambda eng, bank=bank, ws=wslot, it=it, kc=kc:
                                  eng.matmul(
                                      pb[bank][:],
                                      wdq_gu[:, ws * 512 + it * 128:
                                             ws * 512 + (it + 1) * 128],
                                      xn[:, kc * T_PASS:(kc + 1) * T_PASS],
                                      start=(kc == 0), stop=False),
                                  waits=wl2, inc=(it == nit_b - 1))
                    gu_wdq_free[wslot] = v_mm
                    ur = b * 16 + (kc // 4) * 2 + mat
                    us = b * 32 + (kc // 2) * 2 + mat
                    if kc % 4 == 3:
                        raw_cons[ur] = S.val("dve")
                    if kc % 2 == 1:
                        sbc_cons[us] = S.val("dve")
                if kc % 4 == 3:
                    pump_d_chunks(((b * KC + kc + 1) * nd_chunks) // (NB * KC))
            bank_done = {}
            for mat in range(2):
                v_z, zslot = z_v[mat]
                for it in range(nit_b):
                    bank = mat * 4 + it
                    v_mm = pe(lambda eng, bank=bank, zslot=zslot, it=it: eng.matmul(
                        pb[bank][:],
                        zring_gu[0:32, zslot * 512 + it * 128:
                                 zslot * 512 + (it + 1) * 128],
                        xs_bf[:], start=False, stop=True),
                        waits=W(("dve", v_xs), v_z), inc=True)
                    bank_done[bank] = v_mm
                gu_z_free[zslot] = v_mm

            for it in range(nit_b):
                itg = it_g + it
                gbank, ubank = it, 4 + it
                ss = itg % 2
                v_sil = act(lambda eng, gb=gbank, ss=ss: eng.activation(
                    sil[:, ss * 512:(ss + 1) * 512], pb[gb][:], AF.Silu),
                    waits=W(("pe", bank_done[gbank]), ("dve", sil_free[ss])))
                v_silf = afence()
                if DEBUG and p == 0 and b == 21 and it == 1:
                    v_t0 = dve(lambda eng, zs=z_v[0][1]: eng.tensor_copy(
                        dbg_w[0:32, 768:896],
                        zring_gu[0:32, zs * 512 + 128: zs * 512 + 256]),
                        waits=W(("act", v_silf)))
                    v_t1 = dve(lambda eng: eng.tensor_copy(
                        dbg_g[:, 512:1024], pb[1][:]),
                        waits=W(("act", v_silf), ("dve", v_t0)))
                    dma("pl", dbg["d_wdq"], dbg_w[:], waits=W(("dve", v_t1)))
                    dma("pl", dbg["d_g0"], dbg_g[:], waits=W(("dve", v_t1)))
                v_h = dve(lambda eng, ub=ubank, ss=ss, itg=itg: eng.tensor_tensor(
                    h[:, itg * 512:(itg + 1) * 512],
                    sil[:, ss * 512:(ss + 1) * 512], pb[ub][:], ALU.mult),
                    waits=W(("act", v_silf), ("pe", bank_done[ubank])))
                v_h = dfence()
                sil_free[ss] = v_h
                bank_guard[gbank] = [("act", v_sil)]
                bank_guard[ubank] = [("dve", v_h)]
                h_done_it[itg] = v_h
            if DEBUG and p == 0 and b == 21:
                bank_guard[1].append(("dve", S.val("dve")))
            it_g += nit_b
        h_done = S.val("dve")
        if DEBUG and p == 0:
            dma("pl", dbg["d_h2"], h[:], waits=W(("dve", h_done)))

        pump_d_chunks(nd_chunks)
        attn_done = S.val("dve")
        attn_pool_done = S.val("pool")

        for g_ in range(NIT):
            mslot = g_ % 4
            v_m = dma("ad", mring[:, mslot * NIT:(mslot + 1) * NIT],
                      m_in[g_ * 128:(g_ + 1) * 128, :],
                      waits=W(("pe", m_free[mslot])))
            gw = bank_guard[2] if g_ == 0 else []
            v_mm = pe(lambda eng, g_=g_, ms=mslot: eng.matmul(
                pb[2][0:NIT, :], mring[:, ms * NIT:(ms + 1) * NIT],
                h[:, g_ * 512:(g_ + 1) * 512],
                start=(g_ == 0), stop=(g_ == NIT - 1)),
                waits=W(v_m, ("dve", h_done_it[g_])) + gw,
                inc=True)
            m_free[mslot] = v_mm
        v_hs = dve(lambda eng: eng.tensor_copy(hs_bf[:], pb[2][0:NIT, :]),
                   waits=W(("pe", v_mm)))
        v_hs = dfence()
        bank_guard[2] = [("dve", v_hs)]

        nicb = (NIT + 3) // 4
        rawd_cons = {}
        sbcd_cons = {}
        rawd_v = {}
        sbcd_v = {}
        nd_uses = N_OS * nicb

        def emit_rawd(u):
            os2, icb = divmod(u, nicb)
            nic = min(4, NIT - icb * 4)
            rslot = u % 2
            prev = rawd_cons.get(u - 2)
            assert prev is not None or u < 2
            wl = W(("dve", prev if prev is not None else h_done))
            if u < 2 and p == 0:
                wl = wl + W(v_wdp, v_sdx)
            src_ap = wd_perm.rearrange("(a p) n -> p a n", p=128)[
                :, icb * 4: icb * 4 + nic, os2 * 64:(os2 + 1) * 64]
            dst = raw_d[:, rslot * 256: rslot * 256 + nic * 64].rearrange(
                "p (c n) -> p c n", c=nic)
            rawd_v[u] = [dma("ad", dst, src_ap, waits=wl)]

        def emit_sbcd(u):
            os2, icb = divmod(u, nicb)
            nic = min(4, NIT - icb * 4)
            sslot = u % 2
            prev = sbcd_cons.get(u - 2)
            assert prev is not None or u < 2
            wl = W(("dve", prev if prev is not None else h_done))
            if u < 2 and p == 0:
                wl = wl + W(v_sdx)
            src_ap = sd_exp.rearrange("(a p) n -> p a n", p=128)[
                :, icb * 4: icb * 4 + nic, os2 * 512:(os2 + 1) * 512]
            dst = sbc_d[:, sslot * 2048: sslot * 2048 + nic * 512].rearrange(
                "p (c n) -> p c n", c=nic)
            sbcd_v[u] = [dma("sp", dst, src_ap, waits=wl)]

        next_rd = [0]

        def pump_d(u_t):
            while next_rd[0] <= min(u_t, nd_uses - 1):
                emit_rawd(next_rd[0])
                emit_sbcd(next_rd[0])
                next_rd[0] += 1

        for os_ in range(N_OS):
            dnb = [(os_ % 2) * 4 + k for k in range(4)]
            zslot = os_ % 2
            v_zr = dma("sp", zring_d[0:NIT, zslot * 512:(zslot + 1) * 512],
                       zs_d_d[:, os_ * 512:(os_ + 1) * 512],
                       waits=W(("pe", dn_z_free[zslot]),
                               ("dve", h_done if os_ == 0 else 0)), sem="zd")
            for ic in range(NIT):
                icb, idx = ic // 4, ic % 4
                u = os_ * nicb + icb
                pump_d(u + 1)
                rslot = u % 2
                sslot = u % 2
                wslot = ic % 3
                raw16 = raw_d[:, rslot * 256 + idx * 64:
                              rslot * 256 + (idx + 1) * 64].bitcast(U16)
                wl = (W(*rawd_v[u]) + W(*sbcd_v[u]) +
                      W(("pe", dn_wdq_free[wslot])))
                for j in range(4):
                    dve(lambda eng, j=j, r=raw16: eng.tensor_scalar(
                        plane_dn[:, j * 128:(j + 1) * 128], r,
                        4 * j, 15, ALU.logical_shift_right, ALU.bitwise_and),
                        waits=wl)
                    wl = ()
                dfence(3)
                v_mul = dve(lambda eng, ws=wslot, ss=sslot, idx=idx:
                            eng.tensor_tensor(
                                wdq_dn[:, ws * 512:(ws + 1) * 512], plane_dn[:],
                                sbc_d[:, ss * 2048 + idx * 512:
                                      ss * 2048 + (idx + 1) * 512],
                                ALU.mult))
                v_mul = dfence()
                for tt in range(N_TT):
                    bank = dnb[tt]
                    wl2 = W(("dve", v_mul))
                    if ic == 0:
                        wl2 = wl2 + bank_guard[bank] + W(("dve", h_done))
                        bank_guard[bank] = []
                    v_mm = pe(lambda eng, bank=bank, ws=wslot, ic=ic, tt=tt:
                              eng.matmul(
                                  pb[bank][:],
                                  h[:, ic * 512 + tt * 128: ic * 512 + (tt + 1) * 128],
                                  wdq_dn[:, ws * 512:(ws + 1) * 512],
                                  start=(ic == 0), stop=False),
                              waits=wl2, inc=(tt == N_TT - 1))
                dn_wdq_free[wslot] = v_mm
                if idx == 3 or ic == NIT - 1:
                    rawd_cons[u] = S.val("dve")
                    sbcd_cons[u] = S.val("dve")
            for tt in range(N_TT):
                bank = dnb[tt]
                v_mm = pe(lambda eng, bank=bank, tt=tt, zs=zslot: eng.matmul(
                    pb[bank][:],
                    hs_bf[:, tt * 128:(tt + 1) * 128],
                    zring_d[0:NIT, zs * 512:(zs + 1) * 512],
                    start=False, stop=True),
                    waits=W(("dve", v_hs), v_zr), inc=True)
                if tt == N_TT - 1:
                    dn_z_free[zslot] = v_mm
                r0 = t0 + tt * 128
                u2 = (os_ * N_TT + tt) % 2
                v_xl = dma("ad", xsl[:, u2 * 512:(u2 + 1) * 512],
                           x_in[r0:r0 + 128, os_ * 512:(os_ + 1) * 512],
                           waits=W(("dve", xsl_free[u2])))
                v_sil = act(lambda eng, bank=bank, u2=u2: eng.activation(
                    sil[:, u2 * 512:(u2 + 1) * 512], pb[bank][:], AF.Silu),
                    waits=W(("pe", v_mm), ("dve", sil_free[u2])))
                bank_guard[bank] = [("act", v_sil)]
                v_sil = afence()
                v_sx = dve(lambda eng, tt=tt, os_=os_, u2=u2: eng.tensor_copy(
                    sexp[:, u2 * 128:(u2 + 1) * 128].rearrange(
                        "p (d m) -> p d m", m=8),
                    o_attn[:, tt * 128 + os_ * 16: tt * 128 + (os_ + 1) * 16
                           ].unsqueeze(2).broadcast_to([128, 16, 8])),
                    waits=W(("dve", sexp_free[u2]), ("dve", attn_done)))
                v_sx = dfence()
                v_sg = dve(lambda eng, u2=u2: eng.tensor_tensor(
                    sgb[:, u2 * 512:(u2 + 1) * 512].rearrange(
                        "p (j dm) -> p j dm", j=4),
                    sil[:, u2 * 512:(u2 + 1) * 512].rearrange(
                        "p (j dm) -> p j dm", j=4),
                    sexp[:, u2 * 128:(u2 + 1) * 128].unsqueeze(1).broadcast_to(
                        [128, 4, 128]),
                    ALU.mult),
                    waits=W(("act", v_sil), ("dve", v_sx), ("dve", sgb_free[u2])))
                sil_free[u2] = v_sg
                sexp_free[u2] = v_sg
                v_out = dve(lambda eng, u2=u2: eng.tensor_tensor(
                    osl[:, u2 * 512:(u2 + 1) * 512].rearrange(
                        "p (m j) -> p m j", j=4),
                    sgb[:, u2 * 512:(u2 + 1) * 512].rearrange(
                        "p (j m) -> p m j", j=4),
                    xsl[:, u2 * 512:(u2 + 1) * 512].rearrange(
                        "p (m j) -> p m j", j=4),
                    ALU.add),
                    waits=W(v_xl, ("dve", v_sg), osl_free[u2]))
                v_out = dfence()
                xsl_free[u2] = v_out
                sgb_free[u2] = v_out
                v_st = dma("pl", out_t[r0:r0 + 128, os_ * 512:(os_ + 1) * 512],
                           osl[:, u2 * 512:(u2 + 1) * 512],
                           waits=W(("dve", v_out)))
                osl_free[u2] = v_st
        if DEBUG and p == 0:
            dma("pl", dbg["d_xn"], xn[:], waits=W(("dve", S.val("dve"))))
            dma("pl", dbg["d_qkvf"], qkvf[:], waits=W(("dve", S.val("dve"))))
            dma("pl", dbg["d_oattn"], o_attn[:], waits=W(("dve", attn_done)))
            dma("pl", dbg["d_stat"], stat[:], waits=W(("dve", S.val("dve"))))
            dma("pl", dbg["d_xs"], xs_bf[:], waits=W(("dve", S.val("dve"))))
            dma("pl", dbg["d_hs"], hs_bf[:], waits=W(("dve", S.val("dve"))))
            dma("pl", dbg["d_h"], h[:], waits=W(("dve", S.val("dve"))))
        prev_pass_end = S.snap()

    S.emit("sp", lambda eng: eng.nop(),
           waits=snap_waits(S.snap(), ["pl", "ad", "sp", "sd", "sd2"]))

    with nc.Block() as block:
        def make_runner(name):
            dma_sems = set(DMA_SEM_NAMES)

            def run(eng):
                last = {}
                for (fn, waits, inc, inc_n) in S.ev[name]:
                    for (sn, val) in waits:
                        if last.get(sn, 0) >= val:
                            continue
                        eng.wait_ge(sems[sn], val * DMAW if sn in dma_sems else val)
                        last[sn] = val
                    ins = fn(eng)
                    if inc is not None:
                        ins.then_inc(sems[inc],
                                     DMAW if inc in dma_sems else inc_n)
            return run

        block.sync(make_runner("sp"))
        block.tensor(make_runner("pe"))
        block.vector(make_runner("dve"))
        block.scalar(make_runner("act"))
        block.gpsimd(make_runner("pool"))

    ctx.close()
    return nc


_CACHE = {}


def _consts():
    ident = np.eye(128, dtype=ml_dtypes.bfloat16)
    oneh = np.zeros((128, 32 * 32), dtype=ml_dtypes.bfloat16)
    for kc in range(32):
        oneh[:, kc * 32 + kc] = 1.0
    m = np.zeros((I_DIM, NIT), dtype=ml_dtypes.bfloat16)
    for r in range(I_DIM):
        m[r, _cp_nat(r) // 128] = 1.0
    return ident, oneh, m


def kernel(L0_attn_x, L0_attn_Wq, L0_attn_bq, L0_attn_Wk, L0_attn_bk,
           L0_attn_Wv, L0_attn_bv,
           L0_ffn_gate_qweight, L0_ffn_gate_scales, L0_ffn_gate_qzeros,
           L0_ffn_up_qweight, L0_ffn_up_scales, L0_ffn_up_qzeros,
           L0_ffn_down_qweight, L0_ffn_down_scales, L0_ffn_down_qzeros):
    x = np.ascontiguousarray(np.asarray(L0_attn_x, dtype=np.float32)).reshape(8192, D)
    wq = np.ascontiguousarray(np.asarray(L0_attn_Wq, dtype=np.float32)).reshape(D, D)
    wk = np.ascontiguousarray(np.asarray(L0_attn_Wk, dtype=np.float32)).reshape(D, 128)
    wv = np.ascontiguousarray(np.asarray(L0_attn_Wv, dtype=np.float32)).reshape(D, 128)
    bq = np.ascontiguousarray(np.asarray(L0_attn_bq, dtype=np.float32)).reshape(128, 32)
    bk = np.ascontiguousarray(np.asarray(L0_attn_bk, dtype=np.float32)).reshape(1, 128)
    bv = np.ascontiguousarray(np.asarray(L0_attn_bv, dtype=np.float32)).reshape(1, 128)
    gqw = np.ascontiguousarray(np.asarray(L0_ffn_gate_qweight, dtype=np.int32))
    gsc = np.ascontiguousarray(np.asarray(L0_ffn_gate_scales, dtype=np.float32))
    gqz = np.ascontiguousarray(np.asarray(L0_ffn_gate_qzeros, dtype=np.int32))
    uqw = np.ascontiguousarray(np.asarray(L0_ffn_up_qweight, dtype=np.int32))
    usc = np.ascontiguousarray(np.asarray(L0_ffn_up_scales, dtype=np.float32))
    uqz = np.ascontiguousarray(np.asarray(L0_ffn_up_qzeros, dtype=np.int32))
    dqw = np.ascontiguousarray(np.asarray(L0_ffn_down_qweight, dtype=np.int32))
    dsc = np.ascontiguousarray(np.asarray(L0_ffn_down_scales, dtype=np.float32))
    dqz = np.ascontiguousarray(np.asarray(L0_ffn_down_qzeros, dtype=np.int32))

    ident, oneh, m = _consts()
    if "nc" not in _CACHE:
        _CACHE["nc"] = build_nc()
    nc = _CACHE["nc"]
    in_maps = []
    for c in range(N_CORES):
        in_maps.append(dict(
            x=x[c * T_CORE:(c + 1) * T_CORE], wq=wq, wk=wk, wv=wv,
            bq=bq, bk=bk, bv=bv,
            g_qw=gqw, g_sc=gsc, g_qz=gqz,
            u_qw=uqw, u_sc=usc, u_qz=uqz,
            d_qw=dqw, d_sc=dsc, d_qz=dqz,
            ident=ident, onehot=oneh, mmat=m,
        ))
    res = run_bass_kernel_spmd(nc, in_maps, core_ids=list(range(N_CORES)))
    out = np.concatenate([res.results[c]["out"] for c in range(N_CORES)], axis=0)
    return out.reshape(4, 2048, D)
